# version 01
# baseline (speedup 1.0000x reference)
import numpy as np
import ml_dtypes
import concourse.bass as bass
import concourse.mybir as mybir
from concourse.bass_utils import run_bass_kernel_spmd

F32 = mybir.dt.float32
BF16 = mybir.dt.bfloat16
ACTF = mybir.ActivationFunctionType
ALU = mybir.AluOpType

S = 8192
D = 1024
NT = S // 128
NST = S // 512
NG = NT // 2
NE = 16
LN_EPS = 1e-5

BF_COLS = 70400
FP_COLS = 13184

_ident_holder = [None]
_consts_holder = [None]


class _IdentProxy:
    def __getitem__(self, sl):
        return _ident_holder[0][sl]


identh = _IdentProxy()


def epsv():
    return _consts_holder[0][:, 0:1]


class SemP:
    __slots__ = ("name", "n")

    def __init__(self, name):
        self.name = name
        self.n = 0


class Plan:
    def __init__(self):
        self.ops = []
        self.sems = []

    def sem(self, name):
        s = SemP(name)
        self.sems.append(s)
        return s

    def op(self, engine, fn, waits=(), incs=()):
        for s, v in waits:
            assert v <= s.n, f"wait on future value of {s.name}: {v} > {s.n}"
        entry = (engine, fn, tuple(waits), tuple(incs))
        for s, k in incs:
            s.n += k
        self.ops.append(entry)


class Arena:
    def __init__(self, cols):
        self.cols = cols
        self.off = 0
        self.h = None

    def alloc(self, n):
        n = (n + 127) & ~127
        a = self.off
        self.off += n
        assert self.off <= self.cols, f"arena overflow: {self.off} > {self.cols}"
        return a

    def reset(self):
        self.off = 0


def build(stop_after="full", debug_taps=False):
    nc = bass.Bass("TRN2", target_bir_lowering=False)
    dbg = nc.dram_tensor("dbg", [128, 16384], F32, kind="ExternalOutput").ap() \
        if debug_taps else None

    x_in = nc.dram_tensor("x", [S, D], F32, kind="ExternalInput").ap()
    WqT = nc.dram_tensor("WqT", [D, D], F32, kind="ExternalInput").ap()
    WkT = nc.dram_tensor("WkT", [D, D], F32, kind="ExternalInput").ap()
    WvT = nc.dram_tensor("WvT", [D, D], F32, kind="ExternalInput").ap()
    bq = nc.dram_tensor("bq", [1, D], F32, kind="ExternalInput").ap()
    bk = nc.dram_tensor("bk", [1, D], F32, kind="ExternalInput").ap()
    bv = nc.dram_tensor("bv", [1, D], F32, kind="ExternalInput").ap()
    esel = nc.dram_tensor("esel", [NE, D], F32, kind="ExternalInput").ap()
    keys = nc.dram_tensor("keys", [NE, D, 128], F32, kind="ExternalInput").ap()
    values = nc.dram_tensor("values", [NE, 128, D], F32, kind="ExternalInput").ap()
    identd = nc.dram_tensor("ident", [128, 128], BF16, kind="ExternalInput").ap()
    out = nc.dram_tensor("out", [S, D], F32, kind="ExternalOutput").ap()
    x1d = nc.dram_tensor("x1d", [S, D], F32, kind="Internal").ap()
    x2d = nc.dram_tensor("x2d", [S, D], F32, kind="Internal").ap()
    qtd = nc.dram_tensor("qtd", [8, 128, S], BF16, kind="Internal").ap()
    ktd = nc.dram_tensor("ktd", [8, 128, S], BF16, kind="Internal").ap()
    vd = nc.dram_tensor("vd", [S, D], BF16, kind="Internal").ap()

    P = Plan()
    sLD = P.sem("sLD")
    sST = P.sem("sST")
    sGW = P.sem("sGW")
    sT = P.sem("sT")
    sV = P.sem("sV")
    sSc = P.sem("sSc")
    sG = P.sem("sG")

    ab = Arena(BF_COLS)
    af = Arena(FP_COLS)
    ps = {}

    def bfv(a, n, p0=0, p1=128):
        return lambda: ab.h[p0:p1, a:a + n]

    def bfr(a, n, pat, **kw):
        return lambda: ab.h[0:128, a:a + n].rearrange(pat, **kw)

    def fpr(a, n, pat, **kw):
        return lambda: af.h[0:128, a:a + n].rearrange(pat, **kw)

    def fpv(a, n, p0=0, p1=128):
        return lambda: af.h[p0:p1, a:a + n]

    def psv(name, c0, c1):
        return lambda: ps[name][:, c0:c1]

    def DMA(o, i):
        return lambda e: e.dma_start(out=o(), in_=i())

    def MM(o, l, r, st, sp):
        return lambda e: e.matmul(o(), l(), r(), start=st, stop=sp,
                                  skip_group_check=True)

    def dram(ap):
        return lambda: ap

    def plan_ln(x_v, xn_v, st_a, waits_scalar, waits_xn=()):
        c = lambda i: fpv(st_a + i, 1)
        P.op("scalar", lambda e: e.activation(out=xn_v(), in_=x_v(), func=ACTF.Copy,
                                              accum_out=c(0)()),
             waits=waits_scalar)
        P.op("scalar", lambda e: e.activation(out=xn_v(), in_=x_v(), func=ACTF.Square,
                                              accum_out=c(1)()),
             incs=[(sSc, 1)])
        m_stats = sSc.n
        P.op("vector", lambda e: e.tensor_scalar(out=c(2)(), in0=c(0)(), scalar1=1.0 / D,
                                                 scalar2=None, op0=ALU.mult),
             waits=[(sSc, m_stats)])
        P.op("vector", lambda e: e.tensor_scalar(out=c(3)(), in0=c(1)(), scalar1=1.0 / D,
                                                 scalar2=None, op0=ALU.mult))
        P.op("vector", lambda e: e.tensor_tensor(out=c(4)(), in0=c(2)(), in1=c(2)(),
                                                 op=ALU.mult))
        P.op("vector", lambda e: e.drain())
        P.op("vector", lambda e: e.tensor_tensor(out=c(5)(), in0=c(3)(), in1=c(4)(),
                                                 op=ALU.subtract), incs=[(sV, 1)])
        m_var = sV.n
        P.op("scalar", lambda e: e.activation(out=c(6)(), in_=c(5)(), func=ACTF.Sqrt,
                                              bias=epsv()), waits=[(sV, m_var), (sG, 1)],
             incs=[(sSc, 1)])
        m_std = sSc.n
        P.op("vector", lambda e: e.reciprocal(out=c(7)(), in_=c(6)()),
             waits=[(sSc, m_std)])
        P.op("vector", lambda e: e.drain())
        P.op("vector", lambda e: e.tensor_scalar(out=xn_v(), in0=x_v(), scalar1=c(2)(),
                                                 scalar2=c(7)(), op0=ALU.subtract,
                                                 op1=ALU.mult),
             waits=list(waits_xn), incs=[(sV, 1)])
        return sV.n

    def plan_ffn_phase(tag, src_ap, dst_ap, ew):
        ab.reset()
        af.reset()
        keys_a = ab.alloc(NE * 128 * 8)
        vals_a = ab.alloc(NE * D)
        esn_a = ab.alloc(D)
        esT_a = ab.alloc(128)
        xnbf_a = [ab.alloc(D) for _ in range(2)]
        xnT_a = [ab.alloc(D) for _ in range(2)]
        hp_a = [ab.alloc(2048) for _ in range(2)]
        hpT_a = [ab.alloc(2048) for _ in range(2)]
        xb_a = [af.alloc(D) for _ in range(2)]
        x1s_a = [af.alloc(D) for _ in range(2)]
        st_a = [af.alloc(8) for _ in range(2)]
        selb_a = [af.alloc(16) for _ in range(2)]

        for dc in range(8):
            P.op("gpsimd", DMA(bfr(keys_a + dc * 2048, 2048, "p (e h) -> p e h", e=NE),
                               lambda dc=dc: keys[:, dc * 128:(dc + 1) * 128, :]
                               .rearrange("e p h -> p e h")),
                 waits=ew.get("gpsimd", ()) if dc == 0 else (), incs=[(sGW, 16)])
        P.op("gpsimd", DMA(bfr(vals_a, NE * D, "p (e v) -> p e v", e=NE),
                           lambda: values.rearrange("e p v -> p e v")),
             incs=[(sGW, 16)])
        P.op("gpsimd", DMA(bfv(esn_a, D, 0, NE), dram(esel)), incs=[(sGW, 16)])
        m_wload = sGW.n
        P.op("vector", lambda e: e.tensor_scalar(out=bfv(vals_a, NE * D)(),
                                                 in0=bfv(vals_a, NE * D)(),
                                                 scalar1=0.5, scalar2=None,
                                                 op0=ALU.mult),
             waits=[(sGW, m_wload)] + list(ew.get("vector", ())), incs=[(sV, 1)])
        m_vhalf = sV.n
        for dc in range(8):
            P.op("tensor", MM(psv("tr", dc * 16, dc * 16 + 16),
                              bfv(esn_a + dc * 128, 128, 0, NE),
                              lambda: identh[0:NE, 0:NE], True, True),
                 waits=([(sGW, m_wload)] + list(ew.get("tensor", ()))) if dc == 0 else (),
                 incs=[(sT, 1)] if dc == 7 else ())
        m_esT_mm = sT.n
        P.op("scalar", lambda e: e.activation(out=bfv(esT_a, 128)(),
                                              in_=psv("tr", 0, 128)(), func=ACTF.Copy),
             waits=[(sT, m_esT_mm)] + list(ew.get("scalar", ())), incs=[(sSc, 1)])
        m_esT = sSc.n

        ps_tr_free = [m_esT, m_esT]
        relu_done = [0, 0]
        sel_done = [0, 0]
        resid_done = [0, 0]
        store_done = [0, 0]
        xn_rd = [0, 0]

        for t in range(NT):
            par = t % 2
            lw = []
            if t >= 2:
                lw += [(sV, resid_done[par]), (sST, store_done[par])]
            if t == 0:
                lw += list(ew.get("sync", ()))
            P.op("sync", DMA(fpv(xb_a[par], D), dram(src_ap[t * 128:(t + 1) * 128, :])),
                 waits=lw, incs=[(sLD, 16)])
            m_x = sLD.n
            wxn = [(sT, xn_rd[par])] if t >= 2 else []
            m_xn = plan_ln(fpv(xb_a[par], D), bfv(xnbf_a[par], D), st_a[par],
                           waits_scalar=[(sLD, m_x)], waits_xn=wxn)
            for r in range(2):
                for k in range(4):
                    w = [(sV, m_xn), (sSc, ps_tr_free[r])] if k == 0 else ()
                    P.op("tensor", MM(psv("tr", r * 512 + k * 128, r * 512 + (k + 1) * 128),
                                      bfv(xnbf_a[par] + (4 * r + k) * 128, 128),
                                      lambda: identh[:, :], True, True),
                         waits=w, incs=[(sT, 1)] if k == 3 else ())
                m_tr = sT.n
                P.op("scalar", lambda e, r=r, par=par: e.activation(
                    out=bfv(xnT_a[par] + r * 512, 512)(),
                    in_=psv("tr", r * 512, r * 512 + 512)(), func=ACTF.Copy),
                     waits=[(sT, m_tr)], incs=[(sSc, 1)])
                ps_tr_free[r] = sSc.n
            xn_rd[par] = sT.n
            m_xnT = sSc.n
            w0 = [(sSc, m_xnT)]
            if t >= 1:
                w0 += [(sSc, relu_done[1 - par]), (sSc, sel_done[1 - par])]
            for dc in range(8):
                P.op("tensor", MM(psv("tr", 0, 16),
                                  bfv(xnT_a[par] + dc * 128, 128),
                                  bfv(esT_a + dc * 16, 16), dc == 0, dc == 7),
                     waits=w0 if dc == 0 else ())
                for q in range(4):
                    P.op("tensor", MM(psv("h", q * 512, (q + 1) * 512),
                                      bfv(xnT_a[par] + dc * 128, 128),
                                      bfv(keys_a + dc * 2048 + q * 512, 512),
                                      dc == 0, dc == 7),
                         incs=[(sT, 1)] if (dc == 7 and q == 3) else ())
            m_h = sT.n
            P.op("scalar", lambda e, par=par: e.activation(
                out=fpv(selb_a[par], 16)(), in_=psv("tr", 0, 16)(), func=ACTF.Sigmoid),
                 waits=[(sT, m_h)], incs=[(sSc, 1)])
            sel_done[par] = sSc.n
            P.op("scalar", lambda e: e.drain())
            for ei in range(NE):
                P.op("scalar", lambda e, ei=ei, par=par: e.activation(
                    out=bfv(hp_a[par] + ei * 128, 128)(),
                    in_=psv("h", ei * 128, (ei + 1) * 128)(),
                    func=ACTF.Relu, scale=fpv(selb_a[par] + ei, 1)()),
                     incs=[(sSc, 1)] if ei == NE - 1 else ())
            relu_done[par] = sSc.n
            for r in range(4):
                for k in range(4):
                    ei = 4 * r + k
                    w = [(sSc, relu_done[par]), (sSc, ps_tr_free[r % 2])] if k == 0 else ()
                    P.op("tensor", MM(psv("tr", (r % 2) * 512 + k * 128,
                                          (r % 2) * 512 + (k + 1) * 128),
                                      bfv(hp_a[par] + ei * 128, 128),
                                      lambda: identh[:, :], True, True),
                         waits=w, incs=[(sT, 1)] if k == 3 else ())
                m_tr = sT.n
                P.op("scalar", lambda e, r=r, par=par: e.activation(
                    out=bfv(hpT_a[par] + r * 512, 512)(),
                    in_=psv("tr", (r % 2) * 512, (r % 2) * 512 + 512)(), func=ACTF.Copy),
                     waits=[(sT, m_tr)], incs=[(sSc, 1)])
                ps_tr_free[r % 2] = sSc.n
            m_hpT = sSc.n
            w1 = [(sSc, m_hpT), (sV, m_vhalf)]
            if t >= 1:
                w1 += [(sV, resid_done[1 - par])]
            for ei in range(NE):
                for q in range(2):
                    P.op("tensor", MM(psv("out", q * 512, (q + 1) * 512),
                                      bfv(hpT_a[par] + ei * 128, 128),
                                      bfv(vals_a + ei * 1024 + q * 512, 512),
                                      ei == 0, ei == NE - 1),
                         waits=w1 if (ei == 0 and q == 0) else (),
                         incs=[(sT, 1)] if (ei == NE - 1 and q == 1) else ())
            m_out = sT.n
            if debug_taps and t < 2 and tag in ("A",):
                base = t * 8192
                P.op("gpsimd", DMA(lambda base=base: dbg[:, base:base + 8],
                                   fpv(st_a[par], 8)), waits=[(sT, m_out)],
                     incs=[(sG, 16)])
                P.op("gpsimd", DMA(lambda base=base: dbg[:, base + 8:base + 24],
                                   fpv(selb_a[par], 16)), incs=[(sG, 16)])
                P.op("gpsimd", DMA(lambda base=base: dbg[:, base + 24:base + 1048],
                                   bfv(xnbf_a[par], D)), incs=[(sG, 16)])
                P.op("gpsimd", DMA(lambda base=base: dbg[:, base + 1048:base + 3096],
                                   bfv(hp_a[par], 2048)), incs=[(sG, 16)])
                P.op("gpsimd", DMA(lambda base=base: dbg[:, base + 3096:base + 4120],
                                   bfv(xnT_a[par], D)), incs=[(sG, 16)])
            P.op("vector", lambda e, par=par: e.tensor_tensor(
                out=fpv(x1s_a[par], D)(), in0=psv("out", 0, 1024)(),
                in1=fpv(xb_a[par], D)(), op=ALU.add),
                 waits=[(sT, m_out)] + ([(sST, store_done[par])] if t >= 2 else []),
                 incs=[(sV, 1)])
            resid_done[par] = sV.n
            P.op("scalar", DMA(dram(dst_ap[t * 128:(t + 1) * 128, :]), fpv(x1s_a[par], D)),
                 waits=[(sV, resid_done[par])], incs=[(sST, 16)])
            store_done[par] = sST.n

        return {"tensor": sT.n, "vector": sV.n, "scalar": sSc.n, "store": sST.n}

    def plan_qkv_phase(finA):
        ab.reset()
        af.reset()
        wq_a = ab.alloc(8192)
        wk_a = ab.alloc(8192)
        wv_a = ab.alloc(8192)
        xn4_a = [ab.alloc(4096) for _ in range(2)]
        xnT4_a = [ab.alloc(4096) for _ in range(2)]
        qks_a = [ab.alloc(8192) for _ in range(2)]
        vs_a = [ab.alloc(4096) for _ in range(2)]
        ones_a = ab.alloc(128)
        b1b_a = ab.alloc(3 * D)
        xb4_a = [af.alloc(4096) for _ in range(2)]
        bvt_a = af.alloc(D)
        bqc_a = af.alloc(8)
        bkc_a = af.alloc(8)
        b1_a = af.alloc(3 * D)
        st4_a = [af.alloc(8) for _ in range(2)]

        entry_t = [(sT, finA["tensor"])]
        entry_fv = [(sV, finA["vector"])]
        entry_st = [(sST, finA["store"])]

        P.op("gpsimd", DMA(bfr(wq_a, 8192, "p (ec f) -> p ec f", ec=8),
                           lambda: WqT.rearrange("(ec p) f -> p ec f", p=128)),
             waits=entry_t, incs=[(sGW, 16)])
        P.op("gpsimd", DMA(bfr(wk_a, 8192, "p (ec f) -> p ec f", ec=8),
                           lambda: WkT.rearrange("(ec p) f -> p ec f", p=128)),
             incs=[(sGW, 16)])
        P.op("gpsimd", DMA(bfr(wv_a, 8192, "p (ec f) -> p ec f", ec=8),
                           lambda: WvT.rearrange("(ec p) f -> p ec f", p=128)),
             incs=[(sGW, 16)])
        m_w = sGW.n
        P.op("sync", DMA(fpv(b1_a, D, 0, 1), dram(bq)),
             waits=entry_fv + entry_st, incs=[(sLD, 16)])
        P.op("sync", DMA(fpv(b1_a + D, D, 0, 1), dram(bk)), incs=[(sLD, 16)])
        P.op("sync", DMA(fpv(b1_a + 2 * D, D, 0, 1), dram(bv)), incs=[(sLD, 16)])
        m_b = sLD.n
        P.op("gpsimd", lambda e: e.memset(bfv(ones_a, 128, 0, 1)(), 1.0),
             incs=[(sG, 1)])
        m_ones = sG.n
        P.op("vector", lambda e: e.tensor_copy(bfv(b1b_a, 3 * D, 0, 1)(),
                                               fpv(b1_a, 3 * D, 0, 1)()),
             waits=[(sLD, m_b)] + entry_fv, incs=[(sV, 1)])
        m_b1b = sV.n
        for bi, src_off in enumerate([0, D]):
            for fc in range(8):
                P.op("tensor", MM(psv("tr", bi * 8 + fc, bi * 8 + fc + 1),
                                  bfv(b1b_a + src_off + fc * 128, 128, 0, 1),
                                  bfv(ones_a, 1, 0, 1), True, True),
                     waits=[(sV, m_b1b), (sG, m_ones)] if (bi == 0 and fc == 0) else ())
        for q in range(2):
            P.op("tensor", MM(psv("out", q * 512, (q + 1) * 512),
                              bfv(ones_a, 128, 0, 1),
                              bfv(b1b_a + 2 * D + q * 512, 512, 0, 1), True, True),
                 incs=[(sT, 1)] if q == 1 else ())
        m_bmm = sT.n
        P.op("scalar", lambda e: e.activation(out=fpv(bqc_a, 8)(), in_=psv("tr", 0, 8)(),
                                              func=ACTF.Copy),
             waits=[(sT, m_bmm), (sSc, finA["scalar"])])
        P.op("scalar", lambda e: e.activation(out=fpv(bkc_a, 8)(), in_=psv("tr", 8, 16)(),
                                              func=ACTF.Copy))
        P.op("scalar", lambda e: e.activation(out=fpv(bvt_a, D)(),
                                              in_=psv("out", 0, 1024)(), func=ACTF.Copy),
             incs=[(sSc, 1)])
        m_bias = sSc.n

        ps_tr_free = m_bias
        ps_qk_free = [m_bias, m_bias]
        ps_v_free = 0
        v_store = [0, 0]
        xnT_rd = [0, 0]
        for st in range(NST):
            par = st % 2
            lw = [(sST, finA["store"])]
            if st >= 2:
                lw.append((sT, xnT_rd[par]))
            if st == 0:
                lw += entry_fv
            P.op("sync", DMA(fpr(xb4_a[par], 4096, "p (j v) -> p j v", j=4),
                             lambda st=st: x1d[st * 512:(st + 1) * 512, :]
                             .rearrange("(j p) v -> p j v", p=128)),
                 waits=lw, incs=[(sLD, 16)])
            m_x = sLD.n
            for j in range(4):
                wxn = [(sT, xnT_rd[par])] if st >= 2 else []
                m_xn = plan_ln(fpv(xb4_a[par] + j * 1024, D),
                               bfv(xn4_a[par] + j * 1024, D), st4_a[par],
                               waits_scalar=[(sLD, m_x)], waits_xn=wxn)
                for r in range(2):
                    for k in range(4):
                        dc = 4 * r + k
                        w = [(sV, m_xn), (sSc, ps_tr_free)] if k == 0 else ()
                        P.op("tensor", MM(psv("tr", k * 128, (k + 1) * 128),
                                          bfv(xn4_a[par] + j * 1024 + dc * 128, 128),
                                          lambda: identh[:, :], True, True),
                             waits=w, incs=[(sT, 1)] if k == 3 else ())
                    m_tr = sT.n
                    for k in range(4):
                        dc = 4 * r + k
                        P.op("scalar", lambda e, par=par, j=j, dc=dc, k=k: e.activation(
                            out=bfv(xnT4_a[par] + dc * 512 + j * 128, 128)(),
                            in_=psv("tr", k * 128, (k + 1) * 128)(), func=ACTF.Copy),
                             waits=[(sT, m_tr)] if k == 0 else (),
                             incs=[(sSc, 1)] if k == 3 else ())
                    ps_tr_free = sSc.n
            m_xnT = sSc.n
            for pi, (w_a, bc_a, qoff, dstd) in enumerate(
                    [(wq_a, bqc_a, 0, qtd), (wk_a, bkc_a, 4096, ktd)]):
                for fc in range(8):
                    pp = fc % 2
                    w = [(sSc, ps_qk_free[pp])]
                    if pi == 0 and fc == 0:
                        w += [(sSc, m_xnT), (sGW, m_w)]
                    for ec in range(8):
                        P.op("tensor", MM(psv("h", pp * 512, (pp + 1) * 512),
                                          bfv(w_a + ec * 1024 + fc * 128, 128),
                                          bfv(xnT4_a[par] + ec * 512, 512),
                                          ec == 0, ec == 7),
                             waits=w if ec == 0 else (),
                             incs=[(sT, 1)] if ec == 7 else ())
                    m_mm = sT.n
                    P.op("scalar", lambda e, par=par, fc=fc, pp=pp, qoff=qoff,
                         bc_a=bc_a: e.activation(
                             out=bfv(qks_a[par] + qoff + fc * 512, 512)(),
                             in_=psv("h", pp * 512, (pp + 1) * 512)(),
                             func=ACTF.Identity, bias=fpv(bc_a + fc, 1)()),
                         waits=[(sT, m_mm)], incs=[(sSc, 1)])
                    ps_qk_free[pp] = sSc.n
                m_proj = sSc.n
                P.op("scalar", DMA(lambda st=st, dstd=dstd:
                                   dstd[:, :, st * 512:(st + 1) * 512]
                                   .rearrange("fc p q -> p fc q"),
                                   bfr(qks_a[par] + qoff, 4096, "p (fc q) -> p fc q", fc=8)),
                     waits=[(sSc, m_proj)], incs=[(sST, 16)])
            for j in range(4):
                w = [(sSc, m_xnT)]
                if ps_v_free:
                    w.append((sV, ps_v_free))
                else:
                    w += entry_fv
                for ec in range(8):
                    for q in range(2):
                        P.op("tensor", MM(psv("out", q * 512, (q + 1) * 512),
                                          bfv(xnT4_a[par] + ec * 512 + j * 128, 128),
                                          bfv(wv_a + ec * 1024 + q * 512, 512),
                                          ec == 0, ec == 7),
                             waits=w if (ec == 0 and q == 0) else (),
                             incs=[(sT, 1)] if (ec == 7 and q == 1) else ())
                m_mm = sT.n
                wv2 = [(sT, m_mm), (sSc, m_bias)]
                if st >= 2 and j == 0:
                    wv2.append((sST, v_store[par]))
                P.op("vector", lambda e, par=par, j=j: e.tensor_tensor(
                    out=bfv(vs_a[par] + j * 1024, D)(), in0=psv("out", 0, 1024)(),
                    in1=fpv(bvt_a, D)(), op=ALU.add),
                     waits=wv2, incs=[(sV, 1)])
                ps_v_free = sV.n
            m_v = sV.n
            P.op("scalar", DMA(lambda st=st: vd[st * 512:(st + 1) * 512, :]
                               .rearrange("(j p) v -> p j v", p=128),
                               bfr(vs_a[par], 4096, "p (j v) -> p j v", j=4)),
                 waits=[(sV, m_v)], incs=[(sST, 16)])
            v_store[par] = sST.n
            xnT_rd[par] = sT.n
        return {"tensor": sT.n, "vector": sV.n, "scalar": sSc.n, "store": sST.n}

    def plan_attn_phase(finB1, x2dst):
        ab.reset()
        af.reset()
        kb_a = [ab.alloc(4096) for _ in range(2)]
        vb_a = [ab.alloc(4096) for _ in range(2)]
        qt_a = [ab.alloc(2048) for _ in range(2)]
        sc_a = [[ab.alloc(8192) for _ in range(2)] for _ in range(2)]
        tmp_a = [ab.alloc(8192) for _ in range(2)]
        mx_a = ab.alloc(32)
        x1g_a = [af.alloc(2048) for _ in range(2)]
        x2s_a = [af.alloc(D) for _ in range(2)]
        den_a = [af.alloc(2) for _ in range(2)]
        d2_a = af.alloc(1)
        rec_a = af.alloc(1)

        entry_t = [(sT, finB1["tensor"])]
        entry_v = [(sV, finB1["vector"])]
        entry_sc = [(sSc, finB1["scalar"])]
        entry_st = [(sST, finB1["store"])]

        kld = {}
        vld = {}
        kcons = {}
        vcons = {}
        sc_evac = {}
        stt_done = {}
        trev = {}
        av_done = {}
        epi_done = {}
        x2_store = {}
        qt_ld = {}
        x1_ld = {}
        scores_mm = {}
        ps_s_free = [finB1["scalar"], finB1["scalar"]]
        ps_tr_free = [finB1["scalar"], finB1["scalar"]]
        kcnt = 0
        vcnt = 0

        def plan_topk(g, j):
            sc_v = bfv(sc_a[g % 2][j], 8192)
            tmp_v = bfv(tmp_a[j], 8192)
            P.op("vector", lambda e: e.max(out=bfv(mx_a, 8)(), in_=sc_v()),
                 waits=[(sSc, sc_evac[(g, j)])])
            P.op("vector", lambda e: e.drain())
            wmr = [(sSc, trev[(g - 1, j)])] if g >= 1 else list(entry_v)
            P.op("vector", lambda e: e.match_replace(out=tmp_v(),
                                                     in_to_replace=bfv(mx_a, 8)(),
                                                     in_values=sc_v(), imm_value=0.0),
                 waits=wmr)
            P.op("vector", lambda e: e.drain())
            for rnd in range(1, 4):
                P.op("vector", lambda e, rnd=rnd: e.max(out=bfv(mx_a + rnd * 8, 8)(),
                                                        in_=tmp_v()))
                P.op("vector", lambda e: e.drain())
                if rnd < 3:
                    P.op("vector", lambda e, rnd=rnd: e.match_replace(
                        out=tmp_v(), in_to_replace=bfv(mx_a + rnd * 8, 8)(),
                        in_values=tmp_v(), imm_value=0.0))
                    P.op("vector", lambda e: e.drain())
            P.op("vector", lambda e, g=g, j=j: e.scalar_tensor_tensor(
                out=tmp_v(), in0=sc_v(), scalar=bfv(mx_a + 31, 1)(), in1=sc_v(),
                op0=ALU.is_ge, op1=ALU.mult,
                accum_out=fpv(den_a[g % 2] + j, 1)()),
                 incs=[(sV, 1)])
            stt_done[(g, j)] = sV.n

        def plan_kload(s16):
            nonlocal kcnt
            if kcnt < 2:
                w = entry_t + entry_st
            else:
                w = [(sT, kcons[kcnt - 2])]
            P.op("sync", DMA(bfr(kb_a[kcnt % 2], 4096, "p (fc k) -> p fc k", fc=8),
                             lambda s16=s16: ktd[:, :, s16 * 512:(s16 + 1) * 512]
                             .rearrange("fc p k -> p fc k")),
                 waits=w, incs=[(sLD, 16)])
            kld[kcnt] = sLD.n
            kcnt += 1

        def plan_vload(s16):
            nonlocal vcnt
            if vcnt < 2:
                w = entry_t + entry_st
            else:
                w = [(sT, vcons[vcnt - 2])]
            P.op("sync", DMA(bfr(vb_a[vcnt % 2], 4096, "p (kc v) -> p kc v", kc=4),
                             lambda s16=s16: vd[s16 * 512:(s16 + 1) * 512, :]
                             .rearrange("(kc p) v -> p kc v", p=128)),
                 waits=w, incs=[(sLD, 16)])
            vld[vcnt] = sLD.n
            vcnt += 1

        for g in range(NG + 1):
            if g < NG:
                plan_kload(0)
                plan_kload(1)
                w = entry_t + entry_st if g < 2 else [(sT, scores_mm[g - 2])]
                P.op("sync", DMA(bfr(qt_a[g % 2], 2048, "p (fc q) -> p fc q", fc=8),
                                 lambda g=g: qtd[:, :, g * 256:(g + 1) * 256]
                                 .rearrange("fc p q -> p fc q")),
                     waits=w, incs=[(sLD, 16)])
                qt_ld[g] = sLD.n
                if g < 2:
                    w = entry_v + entry_st
                else:
                    w = [(sV, epi_done[(g - 2, 1)])]
                P.op("sync", DMA(fpr(x1g_a[g % 2], 2048, "p (j v) -> p j v", j=2),
                                 lambda g=g: x1d[g * 256:(g + 1) * 256, :]
                                 .rearrange("(j p) v -> p j v", p=128)),
                     waits=w, incs=[(sLD, 16)])
                x1_ld[g] = sLD.n
            if g >= 1:
                plan_vload(0)
                plan_vload(1)
            if g < NG:
                for s16 in range(16):
                    ks = g * 16 + s16
                    for j in range(2):
                        pp = (s16 * 2 + j) % 2
                        w = [(sLD, kld[ks]), (sSc, ps_s_free[pp]), (sLD, qt_ld[g])]
                        if s16 == 0 and j == 0:
                            if g >= 2:
                                w.append((sT, av_done[g - 2]))
                            else:
                                w += entry_v + entry_sc + entry_st
                        for fc in range(8):
                            P.op("tensor", MM(psv("h", pp * 512, (pp + 1) * 512),
                                              bfv(qt_a[g % 2] + fc * 256 + j * 128, 128),
                                              bfv(kb_a[ks % 2] + fc * 512, 512),
                                              fc == 0, fc == 7),
                                 waits=w if fc == 0 else (),
                                 incs=[(sT, 1)] if fc == 7 else ())
                        if j == 1:
                            kcons[ks] = sT.n
                        m_mm = sT.n
                        P.op("scalar", lambda e, g=g, j=j, s16=s16, pp=pp: e.activation(
                            out=bfv(sc_a[g % 2][j] + s16 * 512, 512)(),
                            in_=psv("h", pp * 512, (pp + 1) * 512)(), func=ACTF.Copy),
                             waits=[(sT, m_mm)], incs=[(sSc, 1)])
                        ps_s_free[pp] = sSc.n
                        sc_evac[(g, j)] = sSc.n
                    if s16 + 2 < 16:
                        plan_kload(s16 + 2)
                scores_mm[g] = sT.n
            if g >= 1:
                for j in range(2):
                    for r in range(16):
                        for k in range(4):
                            kc = 4 * r + k
                            w = []
                            if k == 0:
                                w = [(sV, stt_done[(g - 1, j)]), (sSc, ps_tr_free[r % 2])]
                            P.op("tensor", MM(psv("tr", (r % 2) * 512 + k * 128,
                                                  (r % 2) * 512 + (k + 1) * 128),
                                              bfv(tmp_a[j] + kc * 128, 128),
                                              lambda: identh[:, :], True, True),
                                 waits=w, incs=[(sT, 1)] if k == 3 else ())
                        m_tr = sT.n
                        P.op("scalar", lambda e, g=g, j=j, r=r: e.activation(
                            out=bfv(sc_a[(g - 1) % 2][j] + r * 512, 512)(),
                            in_=psv("tr", (r % 2) * 512, (r % 2) * 512 + 512)(),
                            func=ACTF.Copy),
                             waits=[(sT, m_tr)], incs=[(sSc, 1)])
                        ps_tr_free[r % 2] = sSc.n
                    trev[(g - 1, j)] = sSc.n
            if g >= 1:
                for s16 in range(16):
                    vs = (g - 1) * 16 + s16
                    for j in range(2):
                        w = [(sLD, vld[vs])]
                        if s16 == 0:
                            w.append((sSc, trev[(g - 1, j)]))
                            if g >= 2:
                                w.append((sV, epi_done[(g - 2, j)]))
                            else:
                                w += entry_v
                        for kc4 in range(4):
                            gkc = s16 * 4 + kc4
                            for hf in range(2):
                                if j == 0:
                                    o = psv("out", hf * 512, (hf + 1) * 512)
                                else:
                                    o = psv("h", 1024 + hf * 512, 1024 + (hf + 1) * 512)
                                P.op("tensor", MM(o,
                                                  bfv(sc_a[(g - 1) % 2][j] + gkc * 128, 128),
                                                  bfv(vb_a[vs % 2] + kc4 * 1024 + hf * 512,
                                                      512),
                                                  gkc == 0, gkc == 63),
                                     waits=w if (kc4 == 0 and hf == 0) else (),
                                     incs=[(sT, 1)] if (kc4 == 3 and hf == 1 and j == 1)
                                     else ())
                    vcons[vs] = sT.n
                    if s16 + 2 < 16:
                        plan_vload(s16 + 2)
                av_done[g - 1] = sT.n
            if g < NG:
                plan_topk(g, 0)
            if g >= 1:
                for j in range(2):
                    w = [(sT, av_done[g - 1]), (sLD, x1_ld[g - 1])]
                    if g >= 2:
                        w.append((sST, x2_store[(g - 2, j)]))
                    P.op("vector", lambda e, g=g, j=j: e.tensor_scalar(
                        out=fpv(d2_a, 1)(), in0=fpv(den_a[(g - 1) % 2] + j, 1)(),
                        scalar1=1e-10, scalar2=None, op0=ALU.add), waits=w)
                    P.op("vector", lambda e: e.drain())
                    P.op("vector", lambda e: e.reciprocal(out=fpv(rec_a, 1)(),
                                                          in_=fpv(d2_a, 1)()))
                    P.op("vector", lambda e: e.drain())
                    if j == 0:
                        o = psv("out", 0, 1024)
                    else:
                        o = psv("h", 1024, 2048)
                    P.op("vector", lambda e, g=g, j=j, o=o: e.scalar_tensor_tensor(
                        out=fpv(x2s_a[j], D)(), in0=o(), scalar=fpv(rec_a, 1)(),
                        in1=fpv(x1g_a[(g - 1) % 2] + j * 1024, D)(),
                        op0=ALU.mult, op1=ALU.add),
                         incs=[(sV, 1)])
                    epi_done[(g - 1, j)] = sV.n
                    P.op("scalar", DMA(lambda g=g, j=j:
                                       x2dst[(g - 1) * 256 + j * 128:
                                             (g - 1) * 256 + (j + 1) * 128, :],
                                       fpv(x2s_a[j], D)),
                         waits=[(sV, epi_done[(g - 1, j)])], incs=[(sST, 16)])
                    x2_store[(g - 1, j)] = sST.n
            if g < NG:
                plan_topk(g, 1)

        return {"tensor": sT.n, "vector": sV.n, "scalar": sSc.n, "store": sST.n}

    P.op("gpsimd", lambda e: e.memset(epsv(), LN_EPS), incs=[(sG, 1)])

    no_w = {}
    if stop_after == "ffn1":
        plan_ffn_phase("A", x_in, out, no_w)
    else:
        finA = plan_ffn_phase("A", x_in, x1d, no_w)
        finB1 = plan_qkv_phase(finA)
        if stop_after == "attn":
            plan_attn_phase(finB1, out)
        else:
            finB2 = plan_attn_phase(finB1, x2d)
            entry_C = {
                "gpsimd": ((sT, finB2["tensor"]), (sSc, finB2["scalar"]),
                           (sV, finB2["vector"])),
                "vector": ((sV, finB2["vector"]),),
                "tensor": ((sSc, finB2["scalar"]),),
                "scalar": ((sV, finB2["vector"]),),
                "sync": ((sV, finB2["vector"]), (sST, finB2["store"])),
            }
            plan_ffn_phase("C", x2d, out, entry_C)

    total_stores = sST.n
    P.op("sync", lambda e: e.nop(), waits=[(sST, total_stores)])

    import contextlib
    with contextlib.ExitStack() as ctx:
        ab.h = ctx.enter_context(nc.sbuf_tensor("arena_bf", [128, BF_COLS], BF16))
        af.h = ctx.enter_context(nc.sbuf_tensor("arena_fp", [128, FP_COLS], F32))
        identh_t = ctx.enter_context(nc.sbuf_tensor("identsb", [128, 128], BF16))
        consts_t = ctx.enter_context(nc.sbuf_tensor("constsb", [128, 1], F32))
        _consts_holder[0] = consts_t
        ps["h"] = ctx.enter_context(nc.psum_tensor("ps_h", [128, 2048], F32))
        ps["out"] = ctx.enter_context(nc.psum_tensor("ps_out", [128, 1024], F32))
        ps["tr"] = ctx.enter_context(nc.psum_tensor("ps_tr", [128, 1024], F32))
        semh = {s: ctx.enter_context(nc.semaphore(s.name)) for s in P.sems}
        _ident_holder[0] = identh_t

        with nc.Block() as block:
            def emit_for(eng_name):
                def runner(eng):
                    waited = {}
                    if eng_name == "gpsimd":
                        eng.dma_start(out=identh_t[:, :], in_=identd).then_inc(
                            semh[sGW], 16)
                    if eng_name == "tensor":
                        eng.wait_ge(semh[sGW], 16)
                    for engine, fn, waits, incs in P.ops:
                        if engine != eng_name:
                            continue
                        for s, v in waits:
                            v2 = v + (16 if s is sGW else 0)
                            if waited.get(s.name, -1) < v2:
                                eng.wait_ge(semh[s], v2)
                                waited[s.name] = v2
                        inst = fn(eng)
                        for s, k in incs:
                            inst.then_inc(semh[s], k)
                return runner

            block.sync(emit_for("sync"))
            block.scalar(emit_for("scalar"))
            block.vector(emit_for("vector"))
            block.tensor(emit_for("tensor"))
            block.gpsimd(emit_for("gpsimd"))

    return nc


def kernel(L0_attn_x, L0_attn_Wq, L0_attn_bq, L0_attn_Wk, L0_attn_bk,
           L0_attn_Wv, L0_attn_bv, L0_ffn_expert_sel, L0_ffn_keys, L0_ffn_values):
    x = np.ascontiguousarray(np.asarray(L0_attn_x, dtype=np.float32))
    B = x.shape[0]
    common = {
        "WqT": np.ascontiguousarray(np.asarray(L0_attn_Wq, np.float32).T),
        "WkT": np.ascontiguousarray(np.asarray(L0_attn_Wk, np.float32).T),
        "WvT": np.ascontiguousarray(np.asarray(L0_attn_Wv, np.float32).T),
        "bq": np.ascontiguousarray(np.asarray(L0_attn_bq, np.float32)[None, :]),
        "bk": np.ascontiguousarray(np.asarray(L0_attn_bk, np.float32)[None, :]),
        "bv": np.ascontiguousarray(np.asarray(L0_attn_bv, np.float32)[None, :]),
        "esel": np.ascontiguousarray(np.asarray(L0_ffn_expert_sel, np.float32)),
        "keys": np.ascontiguousarray(np.asarray(L0_ffn_keys, np.float32)),
        "values": np.ascontiguousarray(np.asarray(L0_ffn_values, np.float32)),
        "ident": np.eye(128, dtype=ml_dtypes.bfloat16),
    }
    nc = build("full")
    in_maps = [dict(common, x=np.ascontiguousarray(x[i])) for i in range(B)]
    res = run_bass_kernel_spmd(nc, in_maps, core_ids=list(range(B)))
    return np.stack([np.asarray(res.results[i]["out"]) for i in range(B)], axis=0)


# version 02
# speedup vs baseline: 1.3859x; 1.3859x over previous
import numpy as np
import ml_dtypes
import concourse.bass as bass
import concourse.mybir as mybir
from concourse.bass_utils import run_bass_kernel_spmd

F32 = mybir.dt.float32
BF16 = mybir.dt.bfloat16
ACTF = mybir.ActivationFunctionType
ALU = mybir.AluOpType

S = 8192
D = 1024
NT = S // 128
NST = S // 512
NG = NT // 2
NE = 16
LN_EPS = 1e-5

BF_COLS = 78720
FP_COLS = 13696

_ident_holder = [None]
_consts_holder = [None]


class _IdentProxy:
    def __getitem__(self, sl):
        return _ident_holder[0][sl]


identh = _IdentProxy()


def epsv():
    return _consts_holder[0][:, 0:1]


def eps10v():
    return _consts_holder[0][:, 1:2]


class SemP:
    __slots__ = ("name", "n")

    def __init__(self, name):
        self.name = name
        self.n = 0


class Plan:
    def __init__(self):
        self.ops = []
        self.sems = []

    def sem(self, name):
        s = SemP(name)
        self.sems.append(s)
        return s

    def op(self, engine, fn, waits=(), incs=()):
        for s, v in waits:
            assert v <= s.n, f"wait on future value of {s.name}: {v} > {s.n}"
        entry = (engine, fn, tuple(waits), tuple(incs))
        for s, k in incs:
            s.n += k
        self.ops.append(entry)


class Arena:
    def __init__(self, cols):
        self.cols = cols
        self.off = 0
        self.h = None

    def alloc(self, n):
        n = (n + 127) & ~127
        a = self.off
        self.off += n
        assert self.off <= self.cols, f"arena overflow: {self.off} > {self.cols}"
        return a

    def reset(self):
        self.off = 0


def build(stop_after="full", debug_taps=False):
    nc = bass.Bass("TRN2", target_bir_lowering=False)
    dbg = nc.dram_tensor("dbg", [128, 16384], F32, kind="ExternalOutput").ap() \
        if debug_taps else None
    dbgx = nc.dram_tensor("dbgx", [S, D], F32, kind="ExternalOutput").ap() \
        if debug_taps else None
    dbgq = nc.dram_tensor("dbgq", [8, 128, S], F32, kind="ExternalOutput").ap() \
        if debug_taps else None
    dbgk = nc.dram_tensor("dbgk", [8, 128, S], F32, kind="ExternalOutput").ap() \
        if debug_taps else None
    dbgv = nc.dram_tensor("dbgv", [S, D], F32, kind="ExternalOutput").ap() \
        if debug_taps else None

    x_in = nc.dram_tensor("x", [S, D], F32, kind="ExternalInput").ap()
    WqT = nc.dram_tensor("WqT", [D, D], F32, kind="ExternalInput").ap()
    WkT = nc.dram_tensor("WkT", [D, D], F32, kind="ExternalInput").ap()
    WvT = nc.dram_tensor("WvT", [D, D], F32, kind="ExternalInput").ap()
    bq = nc.dram_tensor("bq", [1, D], F32, kind="ExternalInput").ap()
    bk = nc.dram_tensor("bk", [1, D], F32, kind="ExternalInput").ap()
    bv = nc.dram_tensor("bv", [1, D], F32, kind="ExternalInput").ap()
    esel = nc.dram_tensor("esel", [NE, D], F32, kind="ExternalInput").ap()
    keys = nc.dram_tensor("keys", [NE, D, 128], F32, kind="ExternalInput").ap()
    values = nc.dram_tensor("values", [NE, 128, D], F32, kind="ExternalInput").ap()
    identd = nc.dram_tensor("ident", [128, 128], BF16, kind="ExternalInput").ap()
    out = nc.dram_tensor("out", [S, D], F32, kind="ExternalOutput").ap()
    x1d = nc.dram_tensor("x1d", [S, D], F32, kind="Internal").ap()
    x2d = nc.dram_tensor("x2d", [S, D], F32, kind="Internal").ap()
    qtd = nc.dram_tensor("qtd", [8, 128, S], BF16, kind="Internal").ap()
    ktd = nc.dram_tensor("ktd", [8, 128, S], BF16, kind="Internal").ap()
    vd = nc.dram_tensor("vd", [S, D], BF16, kind="Internal").ap()

    P = Plan()
    sLx = [P.sem(f"sLx{i}") for i in range(3)]
    sLb = P.sem("sLb")
    sLk = [P.sem(f"sLk{i}") for i in range(2)]
    sLv = [P.sem(f"sLv{i}") for i in range(2)]
    sLq = [P.sem(f"sLq{i}") for i in range(2)]
    sLg = [P.sem(f"sLg{i}") for i in range(2)]
    sSx = [P.sem(f"sSx{i}") for i in range(3)]
    sSq = [P.sem(f"sSq{i}") for i in range(2)]
    sSv = [P.sem(f"sSv{i}") for i in range(2)]
    sSw = [P.sem(f"sSw{i}") for i in range(2)]
    sGW = P.sem("sGW")
    sT = P.sem("sT")
    sV = P.sem("sV")
    sSc = P.sem("sSc")
    sG = P.sem("sG")

    ab = Arena(BF_COLS)
    af = Arena(FP_COLS)
    ps = {}

    def bfv(a, n, p0=0, p1=128):
        return lambda: ab.h[p0:p1, a:a + n]

    def bfr(a, n, pat, **kw):
        return lambda: ab.h[0:128, a:a + n].rearrange(pat, **kw)

    def fpr(a, n, pat, **kw):
        return lambda: af.h[0:128, a:a + n].rearrange(pat, **kw)

    def fpv(a, n, p0=0, p1=128):
        return lambda: af.h[p0:p1, a:a + n]

    def psv(name, c0, c1):
        return lambda: ps[name][:, c0:c1]

    def DMA(o, i):
        return lambda e: e.dma_start(out=o(), in_=i())

    def MM(o, l, r, st, sp):
        return lambda e: e.matmul(o(), l(), r(), start=st, stop=sp,
                                  skip_group_check=True)

    def dram(ap):
        return lambda: ap

    def plan_ln(x_v, xn_v, st_a, waits_scalar, waits_xn=(), mid_cb=None):
        c = lambda i: fpv(st_a + i, 1)
        P.op("scalar", lambda e: e.activation(out=xn_v(), in_=x_v(), func=ACTF.Copy,
                                              accum_out=c(0)()),
             waits=waits_scalar)
        P.op("scalar", lambda e: e.activation(out=xn_v(), in_=x_v(), func=ACTF.Square,
                                              accum_out=c(1)()),
             incs=[(sSc, 1)])
        m_stats = sSc.n
        P.op("vector", lambda e: e.tensor_scalar(out=c(2)(), in0=c(0)(), scalar1=1.0 / D,
                                                 scalar2=None, op0=ALU.mult),
             waits=[(sSc, m_stats)])
        P.op("vector", lambda e: e.tensor_scalar(out=c(3)(), in0=c(1)(), scalar1=1.0 / D,
                                                 scalar2=LN_EPS, op0=ALU.mult,
                                                 op1=ALU.add))
        P.op("vector", lambda e: e.tensor_tensor(out=c(4)(), in0=c(2)(), in1=c(2)(),
                                                 op=ALU.mult), incs=[(sV, 1)])
        m_msq = sV.n
        P.op("scalar", lambda e: e.activation(out=c(5)(), in_=c(4)(), func=ACTF.Sqrt,
                                              scale=-1.0, bias=c(3)()),
             waits=[(sV, m_msq)], incs=[(sSc, 1)])
        m_std = sSc.n
        P.op("vector", lambda e: e.reciprocal(out=c(6)(), in_=c(5)()),
             waits=[(sSc, m_std)])
        if mid_cb is not None:
            mid_cb()
        else:
            P.op("vector", lambda e: e.drain())
        P.op("vector", lambda e: e.tensor_scalar(out=xn_v(), in0=x_v(), scalar1=c(2)(),
                                                 scalar2=c(6)(), op0=ALU.subtract,
                                                 op1=ALU.mult),
             waits=list(waits_xn), incs=[(sV, 1)])
        return sV.n

    def plan_ffn_phase(tag, src_ap, dst_ap, ew):
        ab.reset()
        af.reset()
        DEP = 3
        keys_a = ab.alloc(NE * 128 * 8)
        vals_a = ab.alloc(NE * D)
        esn_a = ab.alloc(D)
        esT_a = ab.alloc(128)
        xnbf_a = [ab.alloc(D) for _ in range(DEP)]
        xnT_a = [ab.alloc(D) for _ in range(DEP)]
        hp_a = [ab.alloc(2048) for _ in range(DEP)]
        hpT_a = [ab.alloc(2048) for _ in range(DEP)]
        xb_a = [af.alloc(D) for _ in range(DEP)]
        x1s_a = [af.alloc(D) for _ in range(DEP)]
        st_a = [af.alloc(8) for _ in range(DEP)]
        selb_a = [af.alloc(16) for _ in range(DEP)]

        for dc in range(8):
            P.op("gpsimd", DMA(bfr(keys_a + dc * 2048, 2048, "p (e h) -> p e h", e=NE),
                               lambda dc=dc: keys[:, dc * 128:(dc + 1) * 128, :]
                               .rearrange("e p h -> p e h")),
                 waits=ew.get("gpsimd", ()) if dc == 0 else (), incs=[(sGW, 16)])
        P.op("gpsimd", DMA(bfr(vals_a, NE * D, "p (e v) -> p e v", e=NE),
                           lambda: values.rearrange("e p v -> p e v")),
             incs=[(sGW, 16)])
        P.op("gpsimd", DMA(bfv(esn_a, D, 0, NE), dram(esel)), incs=[(sGW, 16)])
        m_wload = sGW.n
        P.op("vector", lambda e: e.tensor_scalar(out=bfv(vals_a, NE * D)(),
                                                 in0=bfv(vals_a, NE * D)(),
                                                 scalar1=0.5, scalar2=None,
                                                 op0=ALU.mult),
             waits=[(sGW, m_wload)] + list(ew.get("vector", ())), incs=[(sV, 1)])
        m_vhalf = sV.n
        for dc in range(8):
            P.op("tensor", MM(psv("tr", dc * 16, dc * 16 + 16),
                              bfv(esn_a + dc * 128, 128, 0, NE),
                              lambda: identh[0:NE, 0:NE], True, True),
                 waits=([(sGW, m_wload)] + list(ew.get("tensor", ()))) if dc == 0 else (),
                 incs=[(sT, 1)] if dc == 7 else ())
        m_esT_mm = sT.n
        P.op("scalar", lambda e: e.activation(out=bfv(esT_a, 128)(),
                                              in_=psv("tr", 0, 128)(), func=ACTF.Copy),
             waits=[(sT, m_esT_mm)] + list(ew.get("scalar", ())), incs=[(sSc, 1)])
        m_esT = sSc.n

        ps_tr_free = [m_esT, m_esT]
        ps_tr_eng = ["scalar", "scalar"]
        hA_free = [0, 0]
        hB_free = [0, 0]
        sel_free = 0
        out_free = 0
        relu_done = [0] * DEP
        scale_done = [0] * DEP
        resid_done = [0] * DEP
        store_done = [0] * DEP
        xn_rd = [0] * DEP
        hp_rd = [0] * DEP
        xnT_rd = [0] * DEP
        hpT_rd = [0] * DEP
        pend_resid = [None]

        def wait_tr_half(r):
            s = sV if ps_tr_eng[r] == "vector" else sSc
            return (s, ps_tr_free[r])

        def emit_resid():
            if pend_resid[0] is None:
                return
            tt, m_out = pend_resid[0]
            pend_resid[0] = None
            par = tt % DEP
            w = [(sT, m_out)]
            if tt >= DEP:
                w.append((sSx[par], store_done[par]))
            P.op("vector", lambda e, par=par: e.tensor_tensor(
                out=fpv(x1s_a[par], D)(), in0=psv("out", 0, 1024)(),
                in1=fpv(xb_a[par], D)(), op=ALU.add),
                 waits=w, incs=[(sV, 1)])
            resid_done[par] = sV.n
            P.op("scalar", DMA(dram(dst_ap[tt * 128:(tt + 1) * 128, :]),
                               fpv(x1s_a[par], D)),
                 waits=[(sV, resid_done[par])], incs=[(sSx[par], 16)])
            store_done[par] = sSx[par].n

        for t in range(NT):
            par = t % DEP
            lw = []
            if t >= DEP:
                lw += [(sV, resid_done[par]), (sSx[par], store_done[par])]
            if t == 0:
                lw += list(ew.get("sync", ()))
            P.op("sync", DMA(fpv(xb_a[par], D), dram(src_ap[t * 128:(t + 1) * 128, :])),
                 waits=lw, incs=[(sLx[par], 16)])
            m_x = sLx[par].n
            wxn = [(sT, xn_rd[par])] if t >= DEP else []
            m_xn = plan_ln(fpv(xb_a[par], D), bfv(xnbf_a[par], D), st_a[par],
                           waits_scalar=[(sLx[par], m_x)], waits_xn=wxn,
                           mid_cb=emit_resid if pend_resid[0] is not None else None)
            for r in range(2):
                for k in range(4):
                    w = []
                    if k == 0:
                        w = [(sV, m_xn), wait_tr_half(r)]
                        if r == 0 and t >= 1:
                            w.append((sSc, sel_free))
                    P.op("tensor", MM(psv("tr", r * 512 + k * 128, r * 512 + (k + 1) * 128),
                                      bfv(xnbf_a[par] + (4 * r + k) * 128, 128),
                                      lambda: identh[:, :], True, True),
                         waits=w, incs=[(sT, 1)] if k == 3 else ())
                m_tr = sT.n
                wv = [(sT, m_tr)]
                if t >= DEP:
                    wv.append((sT, xnT_rd[par]))
                P.op("vector", lambda e, r=r, par=par: e.tensor_copy(
                    bfv(xnT_a[par] + r * 512, 512)(),
                    psv("tr", r * 512, r * 512 + 512)()),
                     waits=wv, incs=[(sV, 1)])
                ps_tr_free[r] = sV.n
                ps_tr_eng[r] = "vector"
            xn_rd[par] = sT.n
            m_xnT = sV.n
            w0 = [(sV, m_xnT)]
            if t >= 1:
                w0 += [(sSc, hA_free[0]), (sSc, sel_free)]
            for dc in range(8):
                P.op("tensor", MM(psv("tr", 0, 16),
                                  bfv(xnT_a[par] + dc * 128, 128),
                                  bfv(esT_a + dc * 16, 16), dc == 0, dc == 7),
                     waits=w0 if dc == 0 else ())
                for q in range(2):
                    P.op("tensor", MM(psv("h", q * 512, (q + 1) * 512),
                                      bfv(xnT_a[par] + dc * 128, 128),
                                      bfv(keys_a + dc * 2048 + q * 512, 512),
                                      dc == 0, dc == 7),
                         incs=[(sT, 1)] if (dc == 7 and q == 1) else ())
            m_hA = sT.n
            w0b = [(sSc, hB_free[0])] if t >= 1 else []
            for dc in range(8):
                for q in range(2, 4):
                    P.op("tensor", MM(psv("h", q * 512, (q + 1) * 512),
                                      bfv(xnT_a[par] + dc * 128, 128),
                                      bfv(keys_a + dc * 2048 + q * 512, 512),
                                      dc == 0, dc == 7),
                         waits=w0b if (dc == 0 and q == 2) else (),
                         incs=[(sT, 1)] if (dc == 7 and q == 3) else ())
            m_hB = sT.n
            whp = [(sT, m_hA)]
            if t >= DEP:
                whp.append((sT, hp_rd[par]))
            P.op("scalar", lambda e, par=par: e.activation(
                out=fpv(selb_a[par], 16)(), in_=psv("tr", 0, 16)(), func=ACTF.Sigmoid),
                 waits=whp, incs=[(sSc, 1)])
            sel_free = sSc.n
            P.op("scalar", lambda e, par=par: e.activation(
                out=bfv(hp_a[par], 1024)(), in_=psv("h", 0, 1024)(), func=ACTF.Relu),
                 incs=[(sSc, 1)])
            hA_free[0] = sSc.n
            m_reluA = sSc.n
            P.op("scalar", lambda e, par=par: e.activation(
                out=bfv(hp_a[par] + 1024, 1024)(), in_=psv("h", 1024, 2048)(),
                func=ACTF.Relu),
                 waits=[(sT, m_hB)], incs=[(sSc, 1)])
            hB_free[0] = sSc.n
            m_reluB = sSc.n
            m_scaleA = 0
            for ei in range(NE):
                w = []
                if ei == 0:
                    w = [(sSc, m_reluA)]
                elif ei == 8:
                    w = [(sSc, m_reluB)]
                P.op("vector", lambda e, ei=ei, par=par: e.tensor_scalar(
                    out=bfv(hp_a[par] + ei * 128, 128)(),
                    in0=bfv(hp_a[par] + ei * 128, 128)(),
                    scalar1=fpv(selb_a[par] + ei, 1)(), scalar2=None, op0=ALU.mult),
                     waits=w, incs=[(sV, 1)] if ei in (7, NE - 1) else ())
                if ei == 7:
                    m_scaleA = sV.n
            m_scaleB = sV.n
            scale_done[par] = sV.n
            for r in range(4):
                for k in range(4):
                    ei = 4 * r + k
                    w = []
                    if k == 0:
                        w = [(sV, m_scaleA if r < 2 else m_scaleB), wait_tr_half(r % 2)]
                        if r == 0:
                            w.append((sSc, sel_free))
                    P.op("tensor", MM(psv("tr", (r % 2) * 512 + k * 128,
                                          (r % 2) * 512 + (k + 1) * 128),
                                      bfv(hp_a[par] + ei * 128, 128),
                                      lambda: identh[:, :], True, True),
                         waits=w, incs=[(sT, 1)] if k == 3 else ())
                m_tr = sT.n
                wv = [(sT, m_tr)]
                if t >= DEP and r == 0:
                    wv.append((sT, hpT_rd[par]))
                P.op("vector", lambda e, r=r, par=par: e.tensor_copy(
                    bfv(hpT_a[par] + r * 512, 512)(),
                    psv("tr", (r % 2) * 512, (r % 2) * 512 + 512)()),
                     waits=wv, incs=[(sV, 1)])
                ps_tr_free[r % 2] = sV.n
                ps_tr_eng[r % 2] = "vector"
            hp_rd[par] = sT.n
            m_hpT = sV.n
            w1 = [(sV, m_hpT), (sV, m_vhalf)]
            if t >= 1:
                w1 += [(sV, resid_done[(t - 1) % DEP])]
            for ei in range(NE):
                for q in range(2):
                    P.op("tensor", MM(psv("out", q * 512, (q + 1) * 512),
                                      bfv(hpT_a[par] + ei * 128, 128),
                                      bfv(vals_a + ei * 1024 + q * 512, 512),
                                      ei == 0, ei == NE - 1),
                         waits=w1 if (ei == 0 and q == 0) else (),
                         incs=[(sT, 1)] if (ei == NE - 1 and q == 1) else ())
            hpT_rd[par] = sT.n
            m_out = sT.n
            xnT_rd[par] = sT.n
            assert pend_resid[0] is None
            pend_resid[0] = (t, m_out)
        emit_resid()

        return {"tensor": sT.n, "vector": sV.n, "scalar": sSc.n,
                "stores": [(s, s.n) for s in sSx]}

    def plan_qkv_phase(finA):
        ab.reset()
        af.reset()
        wq_a = ab.alloc(8192)
        wk_a = ab.alloc(8192)
        wv_a = ab.alloc(8192)
        xn4_a = [ab.alloc(4096) for _ in range(2)]
        xnT4_a = [ab.alloc(4096) for _ in range(2)]
        qks_a = [ab.alloc(8192) for _ in range(2)]
        vs_a = [ab.alloc(4096) for _ in range(2)]
        ones_a = ab.alloc(128)
        b1b_a = ab.alloc(3 * D)
        xb4_a = [af.alloc(4096) for _ in range(2)]
        bvt_a = af.alloc(D)
        bqc_a = af.alloc(8)
        bkc_a = af.alloc(8)
        b1_a = af.alloc(3 * D)
        st4_a = [[af.alloc(8) for _ in range(4)] for _ in range(2)]

        entry_t = [(sT, finA["tensor"])]
        entry_fv = [(sV, finA["vector"])]
        entry_st = list(finA["stores"])

        P.op("gpsimd", DMA(bfr(wq_a, 8192, "p (ec f) -> p ec f", ec=8),
                           lambda: WqT.rearrange("(ec p) f -> p ec f", p=128)),
             waits=entry_t, incs=[(sGW, 16)])
        P.op("gpsimd", DMA(bfr(wk_a, 8192, "p (ec f) -> p ec f", ec=8),
                           lambda: WkT.rearrange("(ec p) f -> p ec f", p=128)),
             incs=[(sGW, 16)])
        P.op("gpsimd", DMA(bfr(wv_a, 8192, "p (ec f) -> p ec f", ec=8),
                           lambda: WvT.rearrange("(ec p) f -> p ec f", p=128)),
             incs=[(sGW, 16)])
        m_w = sGW.n
        P.op("sync", DMA(fpv(b1_a, D, 0, 1), dram(bq)),
             waits=entry_fv + entry_st, incs=[(sLb, 16)])
        P.op("sync", DMA(fpv(b1_a + D, D, 0, 1), dram(bk)), incs=[(sLb, 16)])
        P.op("sync", DMA(fpv(b1_a + 2 * D, D, 0, 1), dram(bv)), incs=[(sLb, 16)])
        m_b = sLb.n
        P.op("gpsimd", lambda e: e.memset(bfv(ones_a, 128, 0, 1)(), 1.0),
             incs=[(sG, 1)])
        m_ones = sG.n
        P.op("vector", lambda e: e.tensor_copy(bfv(b1b_a, 3 * D, 0, 1)(),
                                               fpv(b1_a, 3 * D, 0, 1)()),
             waits=[(sLb, m_b)] + entry_fv, incs=[(sV, 1)])
        m_b1b = sV.n
        for bi, src_off in enumerate([0, D]):
            for fc in range(8):
                P.op("tensor", MM(psv("tr", bi * 8 + fc, bi * 8 + fc + 1),
                                  bfv(b1b_a + src_off + fc * 128, 128, 0, 1),
                                  bfv(ones_a, 1, 0, 1), True, True),
                     waits=[(sV, m_b1b), (sG, m_ones)] if (bi == 0 and fc == 0) else ())
        for q in range(2):
            P.op("tensor", MM(psv("out", q * 512, (q + 1) * 512),
                              bfv(ones_a, 128, 0, 1),
                              bfv(b1b_a + 2 * D + q * 512, 512, 0, 1), True, True),
                 incs=[(sT, 1)] if q == 1 else ())
        m_bmm = sT.n
        P.op("scalar", lambda e: e.activation(out=fpv(bqc_a, 8)(), in_=psv("tr", 0, 8)(),
                                              func=ACTF.Copy),
             waits=[(sT, m_bmm), (sSc, finA["scalar"])])
        P.op("scalar", lambda e: e.activation(out=fpv(bkc_a, 8)(), in_=psv("tr", 8, 16)(),
                                              func=ACTF.Copy))
        P.op("scalar", lambda e: e.activation(out=fpv(bvt_a, D)(),
                                              in_=psv("out", 0, 1024)(), func=ACTF.Copy),
             incs=[(sSc, 1)])
        m_bias = sSc.n

        ps_tr_free = [(sSc, m_bias), (sSc, m_bias)]
        ps_qk_free = [(sSc, m_bias), (sSc, m_bias)]
        ps_v_free = None
        v_store = [0, 0]
        qk_store = [0, 0]
        xnT_rd = [0, 0]
        for st in range(NST):
            par = st % 2
            lw = list(entry_st)
            if st >= 2:
                lw.append((sT, xnT_rd[par]))
            if st == 0:
                lw += entry_fv
            P.op("sync", DMA(fpr(xb4_a[par], 4096, "p (j v) -> p j v", j=4),
                             lambda st=st: x1d[st * 512:(st + 1) * 512, :]
                             .rearrange("(j p) v -> p j v", p=128)),
                 waits=lw, incs=[(sLx[par], 16)])
            m_x = sLx[par].n
            c = lambda j, i, par=par: fpv(st4_a[par][j] + i, 1)
            xv = lambda j, par=par: fpv(xb4_a[par] + j * 1024, D)
            xnv = lambda j, par=par: bfv(xn4_a[par] + j * 1024, D)
            for j in range(4):
                P.op("scalar", lambda e, j=j, c=c, xv=xv, xnv=xnv: e.activation(out=xnv(j)(), in_=xv(j)(),
                                                           func=ACTF.Copy,
                                                           accum_out=c(j, 0)()),
                     waits=[(sLx[par], m_x)] if j == 0 else ())
                P.op("scalar", lambda e, j=j, c=c, xv=xv, xnv=xnv: e.activation(out=xnv(j)(), in_=xv(j)(),
                                                           func=ACTF.Square,
                                                           accum_out=c(j, 1)()),
                     incs=[(sSc, 1)] if j == 3 else ())
            m_stats = sSc.n
            for j in range(4):
                P.op("vector", lambda e, j=j, c=c: e.tensor_scalar(
                    out=c(j, 2)(), in0=c(j, 0)(), scalar1=1.0 / D, scalar2=None,
                    op0=ALU.mult), waits=[(sSc, m_stats)] if j == 0 else ())
                P.op("vector", lambda e, j=j, c=c: e.tensor_scalar(
                    out=c(j, 3)(), in0=c(j, 1)(), scalar1=1.0 / D, scalar2=LN_EPS,
                    op0=ALU.mult, op1=ALU.add))
            for j in range(4):
                P.op("vector", lambda e, j=j, c=c: e.tensor_tensor(
                    out=c(j, 4)(), in0=c(j, 2)(), in1=c(j, 2)(), op=ALU.mult),
                     incs=[(sV, 1)] if j == 3 else ())
            m_msq = sV.n
            for j in range(4):
                P.op("scalar", lambda e, j=j, c=c: e.activation(
                    out=c(j, 5)(), in_=c(j, 4)(), func=ACTF.Sqrt, scale=-1.0,
                    bias=c(j, 3)()), waits=[(sV, m_msq)] if j == 0 else (),
                     incs=[(sSc, 1)] if j == 3 else ())
            m_std = sSc.n
            for j in range(4):
                P.op("vector", lambda e, j=j, c=c: e.reciprocal(out=c(j, 6)(), in_=c(j, 5)()),
                     waits=[(sSc, m_std)] if j == 0 else ())
            wxn = [(sT, xnT_rd[par])] if st >= 2 else []
            for j in range(4):
                P.op("vector", lambda e, j=j, c=c, xv=xv, xnv=xnv: e.tensor_scalar(
                    out=xnv(j)(), in0=xv(j)(), scalar1=c(j, 2)(), scalar2=c(j, 6)(),
                    op0=ALU.subtract, op1=ALU.mult),
                     waits=wxn if j == 0 else (), incs=[(sV, 1)] if j == 3 else ())
            m_xn = sV.n
            if debug_taps and st == 0:
                for j in range(4):
                    P.op("gpsimd", lambda e, j=j: e.dma_start(
                        out=dbg[:, 15872 + j * 8:15880 + j * 8],
                        in_=fpv(st4_a[par][j], 8)(),
                        allow_slow_non_contiguous=True),
                         waits=[(sV, m_xn)] if j == 0 else (), incs=[(sG, 16)])
                P.op("gpsimd", DMA(lambda: dbg[:, 15000:15512],
                                   bfv(xn4_a[par], 512)), incs=[(sG, 16)])
                mgs = sG.n
                P.op("tensor", lambda e: e.nop(), waits=[(sG, mgs)])
            for j in range(4):
                for r in range(2):
                    for k in range(4):
                        dc = 4 * r + k
                        w = []
                        if k == 0:
                            w = [(sV, m_xn), ps_tr_free[r]]
                        P.op("tensor", MM(psv("tr", (r % 2) * 512 + k * 128,
                                              (r % 2) * 512 + (k + 1) * 128),
                                          bfv(xn4_a[par] + j * 1024 + dc * 128, 128),
                                          lambda: identh[:, :], True, True),
                             waits=w, incs=[(sT, 1)] if k == 3 else ())
                    m_tr = sT.n
                    for k in range(4):
                        dc = 4 * r + k
                        P.op("vector", lambda e, par=par, j=j, dc=dc, k=k, r=r:
                             e.tensor_copy(
                                 bfv(xnT4_a[par] + dc * 512 + j * 128, 128)(),
                                 psv("tr", (r % 2) * 512 + k * 128,
                                     (r % 2) * 512 + (k + 1) * 128)()),
                             waits=[(sT, m_tr)] if k == 0 else (),
                             incs=[(sV, 1)] if k == 3 else ())
                    ps_tr_free[r % 2] = (sV, sV.n)
            m_xnT = sV.n
            for pi, (w_a, bc_a, qoff, dstd) in enumerate(
                    [(wq_a, bqc_a, 0, qtd), (wk_a, bkc_a, 4096, ktd)]):
                for fc in range(8):
                    pp = fc % 2
                    w = [ps_qk_free[pp]]
                    if pi == 0 and fc == 0:
                        w += [(sV, m_xnT), (sGW, m_w)]
                        if st >= 2:
                            w.append((sSq[par], qk_store[par]))
                    for ec in range(8):
                        P.op("tensor", MM(psv("h", pp * 512, (pp + 1) * 512),
                                          bfv(w_a + ec * 1024 + fc * 128, 128),
                                          bfv(xnT4_a[par] + ec * 512, 512),
                                          ec == 0, ec == 7),
                             waits=w if ec == 0 else (),
                             incs=[(sT, 1)] if ec == 7 else ())
                    m_mm = sT.n
                    P.op("scalar", lambda e, par=par, fc=fc, pp=pp, qoff=qoff,
                         bc_a=bc_a: e.activation(
                             out=bfv(qks_a[par] + qoff + fc * 512, 512)(),
                             in_=psv("h", pp * 512, (pp + 1) * 512)(),
                             func=ACTF.Identity, bias=fpv(bc_a + fc, 1)()),
                         waits=[(sT, m_mm)], incs=[(sSc, 1)])
                    ps_qk_free[pp] = (sSc, sSc.n)
                m_proj = sSc.n
                P.op("scalar", DMA(lambda st=st, dstd=dstd:
                                   dstd[:, :, st * 512:(st + 1) * 512]
                                   .rearrange("fc p q -> p fc q"),
                                   bfr(qks_a[par] + qoff, 4096, "p (fc q) -> p fc q",
                                       fc=8)),
                     waits=[(sSc, m_proj)], incs=[(sSq[par], 16)])
                qk_store[par] = sSq[par].n
            for j in range(4):
                w = [(sV, m_xnT)]
                if ps_v_free is not None:
                    w.append(ps_v_free)
                else:
                    w += entry_fv
                for ec in range(8):
                    for q in range(2):
                        P.op("tensor", MM(psv("out", q * 512, (q + 1) * 512),
                                          bfv(xnT4_a[par] + ec * 512 + j * 128, 128),
                                          bfv(wv_a + ec * 1024 + q * 512, 512),
                                          ec == 0, ec == 7),
                             waits=w if (ec == 0 and q == 0) else (),
                             incs=[(sT, 1)] if (ec == 7 and q == 1) else ())
                m_mm = sT.n
                wv2 = [(sT, m_mm), (sSc, m_bias)]
                if st >= 2 and j == 0:
                    wv2.append((sSv[par], v_store[par]))
                P.op("vector", lambda e, par=par, j=j: e.tensor_tensor(
                    out=bfv(vs_a[par] + j * 1024, D)(), in0=psv("out", 0, 1024)(),
                    in1=fpv(bvt_a, D)(), op=ALU.add),
                     waits=wv2, incs=[(sV, 1)])
                ps_v_free = (sV, sV.n)
            m_v = sV.n
            P.op("scalar", DMA(lambda st=st: vd[st * 512:(st + 1) * 512, :]
                               .rearrange("(j p) v -> p j v", p=128),
                               bfr(vs_a[par], 4096, "p (j v) -> p j v", j=4)),
                 waits=[(sV, m_v)], incs=[(sSv[par], 16)])
            v_store[par] = sSv[par].n
            xnT_rd[par] = sT.n
        return {"tensor": sT.n, "vector": sV.n, "scalar": sSc.n,
                "stores": finA["stores"] + [(s, s.n) for s in sSq + sSv]}

    def plan_attn_phase(finB1, x2dst):
        ab.reset()
        af.reset()
        KD = 4
        kb_a = [ab.alloc(4096) for _ in range(KD)]
        vb_a = [ab.alloc(4096) for _ in range(KD)]
        qt_a = [ab.alloc(2048) for _ in range(2)]
        sc_a = [[ab.alloc(8192) for _ in range(2)] for _ in range(2)]
        tmp_a = ab.alloc(8192)
        x1g_a = [af.alloc(2048) for _ in range(2)]
        x2s_a = [af.alloc(D) for _ in range(2)]
        den_a = [af.alloc(2) for _ in range(2)]
        d2_a = [af.alloc(2) for _ in range(2)]
        rec_a = af.alloc(2)
        mx_a = af.alloc(512)

        entry_t = [(sT, finB1["tensor"])]
        entry_v = [(sV, finB1["vector"])]
        entry_sc = [(sSc, finB1["scalar"])]
        entry_st = list(finB1["stores"])

        P.op("gpsimd", lambda e: e.memset(fpv(mx_a, 512)(), 0.0),
             waits=entry_v + entry_sc, incs=[(sG, 1)])
        m_mxz = sG.n

        kld = {}
        vld = {}
        kcons = {}
        vcons = {}
        sc_evac = {}
        mask_done = {}
        trev = {}
        av_done = {}
        epi_done = {}
        x2_store = {}
        qt_ld = {}
        x1_ld = {}
        ps_s_free = [entry_sc[0], entry_sc[0]]
        ps_tr_free = [entry_sc[0], entry_sc[0]]
        kcnt = 0
        vcnt = 0

        def plan_kload(s16):
            nonlocal kcnt
            if kcnt < KD:
                w = entry_t + entry_st
            else:
                w = [(sT, kcons[kcnt - KD])]
            sl = sLk[kcnt % 2]
            P.op("scalar", DMA(bfr(kb_a[kcnt % KD], 4096, "p (fc k) -> p fc k", fc=8),
                             lambda s16=s16: ktd[:, :, s16 * 512:(s16 + 1) * 512]
                             .rearrange("fc p k -> p fc k")),
                 waits=w, incs=[(sl, 16)])
            kld[kcnt] = (sl, sl.n)
            kcnt += 1

        def plan_vload(s16):
            nonlocal vcnt
            if vcnt < KD:
                w = entry_t + entry_st
            else:
                w = [(sT, vcons[vcnt - KD])]
            sl = sLv[vcnt % 2]
            P.op("sync", DMA(bfr(vb_a[vcnt % KD], 4096, "p (kc v) -> p kc v", kc=4),
                             lambda s16=s16: vd[s16 * 512:(s16 + 1) * 512, :]
                             .rearrange("(kc p) v -> p kc v", p=128)),
                 waits=w, incs=[(sl, 16)])
            vld[vcnt] = (sl, sl.n)
            vcnt += 1

        def plan_qtload(g):
            w = entry_t + entry_st if g < 2 else [(sT, scores_mm_d[g - 2])]
            sl = sLq[g % 2]
            P.op("scalar", DMA(bfr(qt_a[g % 2], 2048, "p (fc q) -> p fc q", fc=8),
                             lambda g=g: qtd[:, :, g * 256:(g + 1) * 256]
                             .rearrange("fc p q -> p fc q")),
                 waits=w, incs=[(sl, 16)])
            qt_ld[g] = (sl, sl.n)
            if debug_taps and g == 0:
                P.op("gpsimd", DMA(lambda: dbg[:, 6144:8192], bfv(qt_a[0], 2048)),
                     waits=[qt_ld[0]], incs=[(sG, 16)])
                mgq = sG.n
                P.op("tensor", lambda e: e.nop(), waits=[(sG, mgq)])

        def plan_x1load(g):
            if g < 2:
                w = entry_v + list(finB1["stores"])
            else:
                w = [(sV, epi_done[(g - 2, 1)])]
            sl = sLg[g % 2]
            P.op("sync", DMA(fpr(x1g_a[g % 2], 2048, "p (j v) -> p j v", j=2),
                             lambda g=g: x1d[g * 256:(g + 1) * 256, :]
                             .rearrange("(j p) v -> p j v", p=128)),
                 waits=w, incs=[(sl, 16)])
            x1_ld[g] = (sl, sl.n)

        scores_mm_d = {}

        def plan_scores(g):
            for s16 in range(16):
                ks = g * 16 + s16
                for j in range(2):
                    pp = (s16 * 2 + j) % 2
                    w = [kld[ks], ps_s_free[pp], qt_ld[g]]
                    if s16 == 0 and j == 0:
                        if g >= 2:
                            w.append((sT, av_done[g - 2]))
                        else:
                            w += entry_v + entry_sc + entry_st
                    for fc in range(8):
                        P.op("tensor", MM(psv("h", pp * 512, (pp + 1) * 512),
                                          bfv(qt_a[g % 2] + fc * 256 + j * 128, 128),
                                          bfv(kb_a[ks % KD] + fc * 512, 512),
                                          fc == 0, fc == 7),
                             waits=w if fc == 0 else (),
                             incs=[(sT, 1)] if fc == 7 else ())
                    if j == 1:
                        kcons[ks] = sT.n
                    m_mm = sT.n
                    P.op("scalar", lambda e, g=g, j=j, s16=s16, pp=pp: e.activation(
                        out=bfv(sc_a[g % 2][j] + s16 * 512, 512)(),
                        in_=psv("h", pp * 512, (pp + 1) * 512)(), func=ACTF.Copy),
                         waits=[(sT, m_mm)], incs=[(sSc, 1)])
                    ps_s_free[pp] = (sSc, sSc.n)
                    sc_evac[(g, j)] = sSc.n
                if s16 + KD < 16:
                    plan_kload(s16 + KD)
            scores_mm_d[g] = sT.n

        def plan_topk(g, j):
            sc_v = bfv(sc_a[g % 2][j], 8192)
            tmp_v = bfv(tmp_a, 8192)
            if debug_taps and (g, j) == (21, 1):
                P.op("vector", lambda e: e.nop(), waits=[(sSc, sc_evac[(g, j)])],
                     incs=[(sV, 1)])
                m0 = sV.n
                P.op("gpsimd", DMA(lambda: dbg[:, 8192:15360], bfv(sc_a[g % 2][j], 7168)),
                     waits=[(sV, m0)], incs=[(sG, 16)])
                P.op("gpsimd", DMA(lambda: dbg[:, 0:2048], bfv(qt_a[g % 2], 2048)),
                     incs=[(sG, 16)])
                P.op("gpsimd", DMA(lambda: dbg[:, 2048:6144], bfv(kb_a[0], 4096)),
                     incs=[(sG, 16)])
                mg0 = sG.n
                P.op("vector", lambda e: e.nop(), waits=[(sG, mg0)])
            P.op("vector", lambda e: e.max(out=fpv(mx_a, 8)(), in_=sc_v()),
                 waits=[(sSc, sc_evac[(g, j)]), (sG, m_mxz)])
            P.op("vector", lambda e: e.drain())
            if g == 0 and j == 0:
                wmr = entry_v + entry_st
            elif j == 0:
                wmr = [(sSc, trev[(g - 1, 1)])]
            else:
                wmr = [(sSc, trev[(g, 0)])]
            P.op("vector", lambda e: e.match_replace(out=tmp_v(),
                                                     in_to_replace=fpv(mx_a, 8)(),
                                                     in_values=sc_v(), imm_value=0.0),
                 waits=wmr)
            P.op("vector", lambda e: e.drain())
            if debug_taps and (g, j) == (21, 1):
                P.op("vector", lambda e: e.nop(), incs=[(sV, 1)])
                m1 = sV.n
                P.op("gpsimd", DMA(lambda: dbg[:, 0:7168], bfv(tmp_a, 7168)),
                     waits=[(sV, m1)], incs=[(sG, 16)])
                mg = sG.n
                P.op("vector", lambda e: e.nop(), waits=[(sG, mg)])
            for rnd in range(1, 4):
                P.op("vector", lambda e, rnd=rnd: e.max(out=fpv(mx_a + rnd * 128, 8)(),
                                                        in_=tmp_v()))
                P.op("vector", lambda e: e.drain())
                if rnd < 3:
                    P.op("vector", lambda e, rnd=rnd: e.match_replace(
                        out=tmp_v(), in_to_replace=fpv(mx_a + rnd * 128, 8)(),
                        in_values=tmp_v(), imm_value=0.0))
                    P.op("vector", lambda e: e.drain())
            P.op("vector", lambda e, g=g, j=j: e.tensor_reduce(
                out=fpv(den_a[g % 2] + j, 1)(), in_=fpv(mx_a, 512)(),
                axis=mybir.AxisListType.X, op=ALU.add))
            P.op("vector", lambda e: e.tensor_scalar(
                out=tmp_v(), in0=sc_v(), scalar1=fpv(mx_a + 3 * 128 + 7, 1)(),
                scalar2=None, op0=ALU.is_ge))
            P.op("vector", lambda e: e.tensor_tensor(out=tmp_v(), in0=tmp_v(),
                                                     in1=sc_v(), op=ALU.mult),
                 incs=[(sV, 1)])
            mask_done[(g, j)] = sV.n
            if debug_taps and (g, j) == (21, 999):
                m = mask_done[(g, j)]
                P.op("gpsimd", DMA(lambda: dbg[:, 0:8192], bfv(sc_a[g % 2][j], 8192)),
                     waits=[(sV, m)], incs=[(sG, 16)])
                P.op("gpsimd", DMA(lambda: dbg[:, 8192:15360], bfv(tmp_a, 7168)),
                     incs=[(sG, 16)])
                P.op("gpsimd", DMA(lambda: dbg[:, 15360:15872], fpv(mx_a, 512)),
                     incs=[(sG, 16)])
                P.op("gpsimd", lambda e: e.dma_start(
                    out=dbg[:, 15872:15873], in_=fpv(den_a[g % 2] + j, 1)(),
                    allow_slow_non_contiguous=True), incs=[(sG, 16)])

        def plan_tr(g, j):
            for r in range(16):
                for k in range(4):
                    kc = 4 * r + k
                    w = []
                    if k == 0:
                        w = [(sV, mask_done[(g, j)]), ps_tr_free[r % 2]]
                    P.op("tensor", MM(psv("tr", (r % 2) * 512 + k * 128,
                                          (r % 2) * 512 + (k + 1) * 128),
                                      bfv(tmp_a + kc * 128, 128),
                                      lambda: identh[:, :], True, True),
                         waits=w, incs=[(sT, 1)] if k == 3 else ())
                m_tr = sT.n
                P.op("scalar", lambda e, g=g, j=j, r=r: e.activation(
                    out=bfv(sc_a[g % 2][j] + r * 512, 512)(),
                    in_=psv("tr", (r % 2) * 512, (r % 2) * 512 + 512)(),
                    func=ACTF.Copy),
                     waits=[(sT, m_tr)], incs=[(sSc, 1)])
                ps_tr_free[r % 2] = (sSc, sSc.n)
            trev[(g, j)] = sSc.n

        def plan_av(g):
            for i in range(KD):
                plan_vload(i)
            for s16 in range(16):
                vs = g * 16 + s16
                for j in range(2):
                    w = [vld[vs]]
                    if s16 == 0:
                        w.append((sSc, trev[(g, j)]))
                        if g >= 1:
                            w.append((sV, epi_done[(g - 1, j)]))
                        else:
                            w += entry_v + entry_sc
                    for kc4 in range(4):
                        gkc = s16 * 4 + kc4
                        for hf in range(2):
                            if j == 0:
                                o = psv("out", hf * 512, (hf + 1) * 512)
                            else:
                                o = psv("h", 1024 + hf * 512, 1024 + (hf + 1) * 512)
                            P.op("tensor", MM(o,
                                              bfv(sc_a[g % 2][j] + gkc * 128, 128),
                                              bfv(vb_a[vs % KD] + kc4 * 1024 + hf * 512,
                                                  512),
                                              gkc == 0, gkc == 63),
                                 waits=w if (kc4 == 0 and hf == 0) else (),
                                 incs=[(sT, 1)] if (kc4 == 3 and hf == 1 and j == 1)
                                 else ())
                vcons[vs] = sT.n
                if s16 + KD < 16:
                    plan_vload(s16 + KD)
            av_done[g] = sT.n

        d2_ready = {}

        def plan_epi_d2(g):
            for j in range(2):
                P.op("scalar", lambda e, g=g, j=j: e.activation(
                    out=fpv(d2_a[g % 2] + j, 1)(), in_=fpv(den_a[g % 2] + j, 1)(),
                    func=ACTF.Identity, bias=eps10v()),
                     waits=[(sV, mask_done[(g, j)]), (sG, 2)],
                     incs=[(sSc, 1)] if j == 1 else ())
            d2_ready[g] = sSc.n

        def plan_epi(g):
            m_d2 = d2_ready[g]
            for j in range(2):
                P.op("vector", lambda e, g=g, j=j: e.reciprocal(
                    out=fpv(rec_a + j, 1)(), in_=fpv(d2_a[g % 2] + j, 1)()),
                     waits=[(sSc, m_d2)] if j == 0 else ())
            for j in range(2):
                w = [(sT, av_done[g]), x1_ld[g]]
                if g >= 2:
                    w.append((sSw[j], x2_store[(g - 2, j)]))
                if j == 0:
                    o = psv("out", 0, 1024)
                else:
                    o = psv("h", 1024, 2048)
                P.op("vector", lambda e, g=g, j=j, o=o: e.scalar_tensor_tensor(
                    out=fpv(x2s_a[j], D)(), in0=o(), scalar=fpv(rec_a + j, 1)(),
                    in1=fpv(x1g_a[g % 2] + j * 1024, D)(),
                    op0=ALU.mult, op1=ALU.add),
                     waits=w, incs=[(sV, 1)])
                epi_done[(g, j)] = sV.n
                P.op("scalar", DMA(lambda g=g, j=j:
                                   x2dst[g * 256 + j * 128:g * 256 + (j + 1) * 128, :],
                                   fpv(x2s_a[j], D)),
                     waits=[(sV, epi_done[(g, j)])], incs=[(sSw[j], 16)])
                x2_store[(g, j)] = sSw[j].n

        for g in range(NG + 1):
            if g == 0:
                for i in range(KD):
                    plan_kload(i)
                plan_qtload(0)
                plan_x1load(0)
                plan_scores(0)
            if g >= 1:
                plan_tr(g - 1, 1)
                plan_av(g - 1)
            if g + 1 < NG:
                for i in range(KD):
                    plan_kload(i)
                plan_qtload(g + 1)
            if g < NG:
                plan_topk(g, 0)
                plan_tr(g, 0)
            if g + 1 < NG:
                plan_scores(g + 1)
            if g >= 1:
                plan_epi(g - 1)
            if g + 1 < NG:
                plan_x1load(g + 1)
            if g < NG:
                plan_topk(g, 1)
                plan_epi_d2(g)

        return {"tensor": sT.n, "vector": sV.n, "scalar": sSc.n,
                "stores": [(s, s.n) for s in sSw]}

    P.op("gpsimd", lambda e: e.memset(epsv(), LN_EPS), incs=[(sG, 1)])
    P.op("gpsimd", lambda e: e.memset(eps10v(), 1e-10), incs=[(sG, 1)])

    no_w = {}
    if stop_after == "ffn1":
        plan_ffn_phase("A", x_in, out, no_w)
    else:
        finA = plan_ffn_phase("A", x_in, x1d, no_w)
        finB1 = plan_qkv_phase(finA)
        if stop_after == "attn":
            plan_attn_phase(finB1, out)
        else:
            finB2 = plan_attn_phase(finB1, x2d)
            entry_C = {
                "gpsimd": ((sT, finB2["tensor"]), (sSc, finB2["scalar"]),
                           (sV, finB2["vector"])),
                "vector": ((sV, finB2["vector"]),),
                "tensor": ((sSc, finB2["scalar"]),),
                "scalar": ((sV, finB2["vector"]),),
                "sync": tuple([(sV, finB2["vector"])] + list(finB2["stores"])),
            }
            plan_ffn_phase("C", x2d, out, entry_C)

    if debug_taps:
        wfin = [(s, s.n) for s in sSx + sSq + sSv + sSw if s.n > 0] \
            + [(sT, sT.n), (sV, sV.n)]
        P.op("gpsimd", DMA(lambda: dbgx[:, :], dram(x1d)), waits=wfin,
             incs=[(sG, 16)])
        P.op("gpsimd", DMA(lambda: dbgq[:, :, :], dram(qtd)), incs=[(sG, 16)])
        P.op("gpsimd", DMA(lambda: dbgk[:, :, :], dram(ktd)), incs=[(sG, 16)])
        P.op("gpsimd", DMA(lambda: dbgv[:, :], dram(vd)), incs=[(sG, 16)])
        P.op("sync", lambda e: e.nop(), waits=[(sG, sG.n)])
    P.op("sync", lambda e: e.nop(),
         waits=[(s, s.n) for s in sSx + sSq + sSv + sSw if s.n > 0])

    import contextlib
    with contextlib.ExitStack() as ctx:
        ab.h = ctx.enter_context(nc.sbuf_tensor("arena_bf", [128, BF_COLS], BF16))
        af.h = ctx.enter_context(nc.sbuf_tensor("arena_fp", [128, FP_COLS], F32))
        identh_t = ctx.enter_context(nc.sbuf_tensor("identsb", [128, 128], BF16))
        consts_t = ctx.enter_context(nc.sbuf_tensor("constsb", [128, 2], F32))
        _consts_holder[0] = consts_t
        ps["h"] = ctx.enter_context(nc.psum_tensor("ps_h", [128, 2048], F32))
        ps["out"] = ctx.enter_context(nc.psum_tensor("ps_out", [128, 1024], F32))
        ps["tr"] = ctx.enter_context(nc.psum_tensor("ps_tr", [128, 1024], F32))
        semh = {s: ctx.enter_context(nc.semaphore(s.name)) for s in P.sems}
        _ident_holder[0] = identh_t

        with nc.Block() as block:
            def emit_for(eng_name):
                def runner(eng):
                    waited = {}
                    if eng_name == "gpsimd":
                        eng.dma_start(out=identh_t[:, :], in_=identd).then_inc(
                            semh[sGW], 16)
                    if eng_name == "tensor":
                        eng.wait_ge(semh[sGW], 16)
                    for engine, fn, waits, incs in P.ops:
                        if engine != eng_name:
                            continue
                        for s, v in waits:
                            v2 = v + (16 if s is sGW else 0)
                            if waited.get(s.name, -1) < v2:
                                eng.wait_ge(semh[s], v2)
                                waited[s.name] = v2
                        inst = fn(eng)
                        for s, k in incs:
                            inst.then_inc(semh[s], k)
                return runner

            block.sync(emit_for("sync"))
            block.scalar(emit_for("scalar"))
            block.vector(emit_for("vector"))
            block.tensor(emit_for("tensor"))
            block.gpsimd(emit_for("gpsimd"))

    return nc


def kernel(L0_attn_x, L0_attn_Wq, L0_attn_bq, L0_attn_Wk, L0_attn_bk,
           L0_attn_Wv, L0_attn_bv, L0_ffn_expert_sel, L0_ffn_keys, L0_ffn_values):
    x = np.ascontiguousarray(np.asarray(L0_attn_x, dtype=np.float32))
    B = x.shape[0]
    common = {
        "WqT": np.ascontiguousarray(np.asarray(L0_attn_Wq, np.float32).T),
        "WkT": np.ascontiguousarray(np.asarray(L0_attn_Wk, np.float32).T),
        "WvT": np.ascontiguousarray(np.asarray(L0_attn_Wv, np.float32).T),
        "bq": np.ascontiguousarray(np.asarray(L0_attn_bq, np.float32)[None, :]),
        "bk": np.ascontiguousarray(np.asarray(L0_attn_bk, np.float32)[None, :]),
        "bv": np.ascontiguousarray(np.asarray(L0_attn_bv, np.float32)[None, :]),
        "esel": np.ascontiguousarray(np.asarray(L0_ffn_expert_sel, np.float32)),
        "keys": np.ascontiguousarray(np.asarray(L0_ffn_keys, np.float32)),
        "values": np.ascontiguousarray(np.asarray(L0_ffn_values, np.float32)),
        "ident": np.eye(128, dtype=ml_dtypes.bfloat16),
    }
    nc = build("full")
    in_maps = [dict(common, x=np.ascontiguousarray(x[i])) for i in range(B)]
    res = run_bass_kernel_spmd(nc, in_maps, core_ids=list(range(B)))
    return np.stack([np.asarray(res.results[i]["out"]) for i in range(B)], axis=0)


# version 04
# speedup vs baseline: 1.5725x; 1.1346x over previous
import numpy as np
import ml_dtypes
import concourse.bass as bass
import concourse.mybir as mybir
from concourse.bass_utils import run_bass_kernel_spmd

F32 = mybir.dt.float32
BF16 = mybir.dt.bfloat16
ACTF = mybir.ActivationFunctionType
ALU = mybir.AluOpType

S = 8192
D = 1024
NT = S // 128
NST = S // 512
NG = NT // 2
NE = 16
LN_EPS = 1e-5

BF_COLS = 78720
FP_COLS = 13696

_ident_holder = [None]
_consts_holder = [None]


class _IdentProxy:
    def __getitem__(self, sl):
        return _ident_holder[0][sl]


identh = _IdentProxy()


def epsv():
    return _consts_holder[0][:, 0:1]


def eps10v():
    return _consts_holder[0][:, 1:2]


class SemP:
    __slots__ = ("name", "n")

    def __init__(self, name):
        self.name = name
        self.n = 0


class Plan:
    def __init__(self):
        self.ops = []
        self.sems = []

    def sem(self, name):
        s = SemP(name)
        self.sems.append(s)
        return s

    def op(self, engine, fn, waits=(), incs=()):
        for s, v in waits:
            assert v <= s.n, f"wait on future value of {s.name}: {v} > {s.n}"
        entry = (engine, fn, tuple(waits), tuple(incs))
        for s, k in incs:
            s.n += k
        self.ops.append(entry)


class Arena:
    def __init__(self, cols):
        self.cols = cols
        self.off = 0
        self.h = None

    def alloc(self, n):
        n = (n + 127) & ~127
        a = self.off
        self.off += n
        assert self.off <= self.cols, f"arena overflow: {self.off} > {self.cols}"
        return a

    def reset(self):
        self.off = 0


def build(stop_after="full", debug_taps=False):
    nc = bass.Bass("TRN2", target_bir_lowering=False)
    dbg = nc.dram_tensor("dbg", [128, 16384], F32, kind="ExternalOutput").ap() \
        if debug_taps else None
    dbgx = nc.dram_tensor("dbgx", [S, D], F32, kind="ExternalOutput").ap() \
        if debug_taps else None
    dbgq = nc.dram_tensor("dbgq", [8, 128, S], F32, kind="ExternalOutput").ap() \
        if debug_taps else None
    dbgk = nc.dram_tensor("dbgk", [8, 128, S], F32, kind="ExternalOutput").ap() \
        if debug_taps else None
    dbgv = nc.dram_tensor("dbgv", [S, D], F32, kind="ExternalOutput").ap() \
        if debug_taps else None

    x_in = nc.dram_tensor("x", [S, D], F32, kind="ExternalInput").ap()
    WqT = nc.dram_tensor("WqT", [D, D], F32, kind="ExternalInput").ap()
    WkT = nc.dram_tensor("WkT", [D, D], F32, kind="ExternalInput").ap()
    WvT = nc.dram_tensor("WvT", [D, D], F32, kind="ExternalInput").ap()
    bq = nc.dram_tensor("bq", [1, D], F32, kind="ExternalInput").ap()
    bk = nc.dram_tensor("bk", [1, D], F32, kind="ExternalInput").ap()
    bv = nc.dram_tensor("bv", [1, D], F32, kind="ExternalInput").ap()
    esel = nc.dram_tensor("esel", [NE, D], F32, kind="ExternalInput").ap()
    keys = nc.dram_tensor("keys", [NE, D, 128], F32, kind="ExternalInput").ap()
    values = nc.dram_tensor("values", [NE, 128, D], F32, kind="ExternalInput").ap()
    identd = nc.dram_tensor("ident", [128, 128], BF16, kind="ExternalInput").ap()
    out = nc.dram_tensor("out", [S, D], F32, kind="ExternalOutput").ap()
    x1d = nc.dram_tensor("x1d", [S, D], F32, kind="Internal").ap()
    x2d = nc.dram_tensor("x2d", [S, D], F32, kind="Internal").ap()
    qtd = nc.dram_tensor("qtd", [8, 128, S], BF16, kind="Internal").ap()
    ktd = nc.dram_tensor("ktd", [8, 128, S], BF16, kind="Internal").ap()
    vd = nc.dram_tensor("vd", [S, D], BF16, kind="Internal").ap()

    P = Plan()
    sLx = [P.sem(f"sLx{i}") for i in range(3)]
    sLb = P.sem("sLb")
    sLk = [P.sem(f"sLk{i}") for i in range(2)]
    sLv = [P.sem(f"sLv{i}") for i in range(2)]
    sLq = [P.sem(f"sLq{i}") for i in range(2)]
    sLg = [P.sem(f"sLg{i}") for i in range(2)]
    sSx = [P.sem(f"sSx{i}") for i in range(3)]
    sSq = [P.sem(f"sSq{i}") for i in range(2)]
    sSv = [P.sem(f"sSv{i}") for i in range(2)]
    sSw = [P.sem(f"sSw{i}") for i in range(2)]
    sGW = P.sem("sGW")
    sT = P.sem("sT")
    sV = P.sem("sV")
    sSc = P.sem("sSc")
    sG = P.sem("sG")

    ab = Arena(BF_COLS)
    af = Arena(FP_COLS)
    ps = {}

    def bfv(a, n, p0=0, p1=128):
        return lambda: ab.h[p0:p1, a:a + n]

    def bfr(a, n, pat, **kw):
        return lambda: ab.h[0:128, a:a + n].rearrange(pat, **kw)

    def fpr(a, n, pat, **kw):
        return lambda: af.h[0:128, a:a + n].rearrange(pat, **kw)

    def fpv(a, n, p0=0, p1=128):
        return lambda: af.h[p0:p1, a:a + n]

    def psv(name, c0, c1):
        return lambda: ps[name][:, c0:c1]

    def DMA(o, i):
        return lambda e: e.dma_start(out=o(), in_=i())

    def MM(o, l, r, st, sp):
        return lambda e: e.matmul(o(), l(), r(), start=st, stop=sp,
                                  skip_group_check=True)

    def dram(ap):
        return lambda: ap

    def plan_ln(x_v, xn_v, st_a, waits_vec, dum_v, waits_xn=(), mid_cb=None):
        c = lambda i, st_a=st_a: fpv(st_a + i, 1)
        P.op("vector", lambda e: e.tensor_reduce(out=c(0)(), in_=x_v(),
                                                 axis=mybir.AxisListType.X, op=ALU.add),
             waits=waits_vec)
        P.op("scalar", lambda e: e.activation(out=xn_v(), in_=x_v(), func=ACTF.Square,
                                              accum_out=c(1)()),
             waits=waits_vec, incs=[(sSc, 1)])
        m_sq = sSc.n
        P.op("vector", lambda e: e.tensor_scalar(out=c(3)(), in0=c(1)(), scalar1=1.0 / D,
                                                 scalar2=LN_EPS, op0=ALU.mult,
                                                 op1=ALU.add), waits=[(sSc, m_sq)])
        P.op("vector", lambda e: e.tensor_tensor(out=c(4)(), in0=c(0)(), in1=c(0)(),
                                                 op=ALU.mult))
        P.op("vector", lambda e: e.tensor_scalar(out=c(2)(), in0=c(0)(), scalar1=1.0 / D,
                                                 scalar2=None, op0=ALU.mult),
             incs=[(sV, 1)])
        m_msq = sV.n
        P.op("scalar", lambda e: e.activation(out=c(5)(), in_=c(4)(), func=ACTF.Sqrt,
                                              scale=-1.0 / (D * D), bias=c(3)()),
             waits=[(sV, m_msq)], incs=[(sSc, 1)])
        m_std = sSc.n
        P.op("vector", lambda e: e.reciprocal(out=c(6)(), in_=c(5)()),
             waits=[(sSc, m_std)])
        if mid_cb is not None:
            mid_cb()
        else:
            P.op("vector", lambda e: e.drain())
        P.op("vector", lambda e: e.tensor_scalar(out=xn_v(), in0=x_v(), scalar1=c(2)(),
                                                 scalar2=c(6)(), op0=ALU.subtract,
                                                 op1=ALU.mult),
             waits=list(waits_xn), incs=[(sV, 1)])
        return sV.n

    def plan_ffn_phase(tag, src_ap, dst_ap, ew):
        ab.reset()
        af.reset()
        DEP = 3
        keys_a = ab.alloc(NE * 128 * 8)
        vals_a = ab.alloc(NE * D)
        esn_a = ab.alloc(D)
        esT_a = ab.alloc(128)
        xnbf_a = [ab.alloc(D) for _ in range(DEP)]
        xnT_a = [ab.alloc(D) for _ in range(DEP)]
        hp_a = [ab.alloc(2048) for _ in range(DEP)]
        hpT_a = [ab.alloc(2048) for _ in range(DEP)]
        xb_a = [af.alloc(D) for _ in range(DEP)]
        x1s_a = [af.alloc(D) for _ in range(DEP)]
        st_a = [af.alloc(8) for _ in range(DEP)]
        selb_a = [af.alloc(16) for _ in range(DEP)]

        for dc in range(8):
            P.op("gpsimd", DMA(bfr(keys_a + dc * 2048, 2048, "p (e h) -> p e h", e=NE),
                               lambda dc=dc: keys[:, dc * 128:(dc + 1) * 128, :]
                               .rearrange("e p h -> p e h")),
                 waits=ew.get("gpsimd", ()) if dc == 0 else (), incs=[(sGW, 16)])
        P.op("gpsimd", DMA(bfr(vals_a, NE * D, "p (e v) -> p e v", e=NE),
                           lambda: values.rearrange("e p v -> p e v")),
             incs=[(sGW, 16)])
        P.op("gpsimd", DMA(bfv(esn_a, D, 0, NE), dram(esel)), incs=[(sGW, 16)])
        m_wload = sGW.n
        P.op("vector", lambda e: e.tensor_scalar(out=bfv(vals_a, NE * D)(),
                                                 in0=bfv(vals_a, NE * D)(),
                                                 scalar1=0.5, scalar2=None,
                                                 op0=ALU.mult),
             waits=[(sGW, m_wload)] + list(ew.get("vector", ())), incs=[(sV, 1)])
        m_vhalf = sV.n
        for dc in range(8):
            P.op("tensor", MM(psv("tr", dc * 16, dc * 16 + 16),
                              bfv(esn_a + dc * 128, 128, 0, NE),
                              lambda: identh[0:NE, 0:NE], True, True),
                 waits=([(sGW, m_wload)] + list(ew.get("tensor", ()))) if dc == 0 else (),
                 incs=[(sT, 1)] if dc == 7 else ())
        m_esT_mm = sT.n
        P.op("scalar", lambda e: e.activation(out=bfv(esT_a, 128)(),
                                              in_=psv("tr", 0, 128)(), func=ACTF.Copy),
             waits=[(sT, m_esT_mm)] + list(ew.get("scalar", ())), incs=[(sSc, 1)])
        m_esT = sSc.n

        ps_tr_free = [m_esT, m_esT]
        ps_tr_eng = ["scalar", "scalar"]
        hA_free = [0]
        hB_free = [0]
        sel_free = [m_esT]
        out_free = 0
        relu_done = [0] * DEP
        scale_done = [0] * DEP
        resid_done = [0] * DEP
        store_done = [0] * DEP
        xn_rd = [0] * DEP
        hp_rd = [0] * DEP
        xnT_rd = [0] * DEP
        hpT_rd = [0] * DEP
        pend_resid = [None]

        def wait_tr_half(r):
            s = sV if ps_tr_eng[r] == "vector" else sSc
            return (s, ps_tr_free[r])

        def emit_resid():
            if pend_resid[0] is None:
                return
            tt, m_out = pend_resid[0]
            pend_resid[0] = None
            par = tt % DEP
            w = [(sT, m_out)]
            if tt >= DEP:
                w.append((sSx[par], store_done[par]))
            P.op("vector", lambda e, par=par: e.tensor_tensor(
                out=fpv(x1s_a[par], D)(), in0=psv("out", 0, 1024)(),
                in1=fpv(xb_a[par], D)(), op=ALU.add),
                 waits=w, incs=[(sV, 1)])
            resid_done[par] = sV.n
            P.op("scalar", DMA(dram(dst_ap[tt * 128:(tt + 1) * 128, :]),
                               fpv(x1s_a[par], D)),
                 waits=[(sV, resid_done[par])], incs=[(sSx[par], 16)])
            store_done[par] = sSx[par].n

        ln_done = {}
        m_xnT_d = {}
        m_hpT_d = {}
        m_out_d = {}
        scale_d = {}

        def plan_front(t):
            par = t % DEP
            emit_resid()
            lw = []
            if t >= DEP:
                lw += [(sV, resid_done[par]), (sSx[par], store_done[par])]
            if t == 0:
                lw += list(ew.get("sync", ()))
            P.op("sync", DMA(fpv(xb_a[par], D), dram(src_ap[t * 128:(t + 1) * 128, :])),
                 waits=lw, incs=[(sLx[par], 16)])
            m_x = sLx[par].n
            wxn = [(sT, xn_rd[par])] if t >= DEP else []
            ln_done[t] = plan_ln(fpv(xb_a[par], D), bfv(xnbf_a[par], D), st_a[par],
                                 waits_vec=[(sLx[par], m_x)], dum_v=fpv(x1s_a[par], D),
                                 waits_xn=wxn)

        def plan_xnT(t):
            par = t % DEP
            for r in range(2):
                for k in range(4):
                    w = []
                    if k == 0:
                        w = [(sV, ln_done[t]), wait_tr_half(r)]
                        if r == 0 and t >= 1:
                            w.append((sSc, sel_free[0]))
                    P.op("tensor", MM(psv("tr", r * 512 + k * 128, r * 512 + (k + 1) * 128),
                                      bfv(xnbf_a[par] + (4 * r + k) * 128, 128),
                                      lambda: identh[:, :], True, True),
                         waits=w, incs=[(sT, 1)] if k == 3 else ())
                m_tr = sT.n
                wv = [(sT, m_tr)]
                if t >= DEP:
                    wv.append((sT, xnT_rd[par]))
                P.op("scalar", lambda e, r=r, par=par: e.activation(
                    out=bfv(xnT_a[par] + r * 512, 512)(),
                    in_=psv("tr", r * 512, r * 512 + 512)(), func=ACTF.Copy),
                     waits=wv, incs=[(sSc, 1)])
                ps_tr_free[r] = sSc.n
                ps_tr_eng[r] = "scalar"
            xn_rd[par] = sT.n
            m_xnT_d[t] = sSc.n

        def plan_selh(t):
            par = t % DEP
            w0 = [(sSc, m_xnT_d[t])]
            if t >= 1:
                w0 += [(sV, hA_free[0]), (sSc, sel_free[0])]
            for dc in range(8):
                P.op("tensor", MM(psv("tr", 0, 16),
                                  bfv(xnT_a[par] + dc * 128, 128),
                                  bfv(esT_a + dc * 16, 16), dc == 0, dc == 7),
                     waits=w0 if dc == 0 else ())
                for q in range(2):
                    P.op("tensor", MM(psv("h", q * 512, (q + 1) * 512),
                                      bfv(xnT_a[par] + dc * 128, 128),
                                      bfv(keys_a + dc * 2048 + q * 512, 512),
                                      dc == 0, dc == 7),
                         incs=[(sT, 1)] if (dc == 7 and q == 1) else ())
            m_hA = sT.n
            w0b = [(sV, hB_free[0])] if t >= 1 else []
            for dc in range(8):
                for q in range(2, 4):
                    P.op("tensor", MM(psv("h", q * 512, (q + 1) * 512),
                                      bfv(xnT_a[par] + dc * 128, 128),
                                      bfv(keys_a + dc * 2048 + q * 512, 512),
                                      dc == 0, dc == 7),
                         waits=w0b if (dc == 0 and q == 2) else (),
                         incs=[(sT, 1)] if (dc == 7 and q == 3) else ())
            m_hB = sT.n
            P.op("scalar", lambda e, par=par: e.activation(
                out=fpv(selb_a[par], 16)(), in_=psv("tr", 0, 16)(), func=ACTF.Sigmoid),
                 waits=[(sT, m_hA)], incs=[(sSc, 1)])
            sel_free[0] = sSc.n
            m_scaleA = 0
            for ei in range(NE):
                w = []
                if ei == 0:
                    w = [(sT, m_hA), (sSc, sel_free[0])]
                    if t >= DEP:
                        w.append((sT, hp_rd[par]))
                elif ei == 8:
                    w = [(sT, m_hB)]
                P.op("vector", lambda e, ei=ei, par=par: e.tensor_scalar(
                    out=bfv(hp_a[par] + ei * 128, 128)(),
                    in0=psv("h", ei * 128, (ei + 1) * 128)(),
                    scalar1=0.0, scalar2=fpv(selb_a[par] + ei, 1)(),
                    op0=ALU.max, op1=ALU.mult),
                     waits=w, incs=[(sV, 1)] if ei in (7, NE - 1) else ())
                if ei == 7:
                    m_scaleA = sV.n
            hA_free[0] = m_scaleA
            hB_free[0] = sV.n
            scale_d[t] = (m_scaleA, sV.n)

        def plan_hpT(t):
            par = t % DEP
            m_scaleA, m_scaleB = scale_d[t]
            for r in range(4):
                for k in range(4):
                    ei = 4 * r + k
                    w = []
                    if k == 0:
                        w = [(sV, m_scaleA if r < 2 else m_scaleB), wait_tr_half(r % 2)]
                        if r == 0:
                            w.append((sSc, sel_free[0]))
                    P.op("tensor", MM(psv("tr", (r % 2) * 512 + k * 128,
                                          (r % 2) * 512 + (k + 1) * 128),
                                      bfv(hp_a[par] + ei * 128, 128),
                                      lambda: identh[:, :], True, True),
                         waits=w, incs=[(sT, 1)] if k == 3 else ())
                m_tr = sT.n
                wv = [(sT, m_tr)]
                if t >= DEP and r == 0:
                    wv.append((sT, hpT_rd[par]))
                P.op("scalar", lambda e, r=r, par=par: e.activation(
                    out=bfv(hpT_a[par] + r * 512, 512)(),
                    in_=psv("tr", (r % 2) * 512, (r % 2) * 512 + 512)(), func=ACTF.Copy),
                     waits=wv, incs=[(sSc, 1)])
                ps_tr_free[r % 2] = sSc.n
                ps_tr_eng[r % 2] = "scalar"
            hp_rd[par] = sT.n
            m_hpT_d[t] = sSc.n

        def plan_out(t):
            par = t % DEP
            w1 = [(sSc, m_hpT_d[t]), (sV, m_vhalf)]
            if t >= 1:
                w1 += [(sV, resid_done[(t - 1) % DEP])]
            for ei in range(NE):
                for q in range(2):
                    P.op("tensor", MM(psv("out", q * 512, (q + 1) * 512),
                                      bfv(hpT_a[par] + ei * 128, 128),
                                      bfv(vals_a + ei * 1024 + q * 512, 512),
                                      ei == 0, ei == NE - 1),
                         waits=w1 if (ei == 0 and q == 0) else (),
                         incs=[(sT, 1)] if (ei == NE - 1 and q == 1) else ())
            hpT_rd[par] = sT.n
            m_out_d[t] = sT.n
            xnT_rd[par] = sT.n
            assert pend_resid[0] is None
            pend_resid[0] = (t, m_out_d[t])

        plan_front(0)
        plan_xnT(0)
        for t in range(NT):
            if t + 1 < NT:
                plan_front(t + 1)
            else:
                emit_resid()
            if t >= 1:
                plan_out(t - 1)
            plan_selh(t)
            if t + 1 < NT:
                plan_xnT(t + 1)
            plan_hpT(t)
        emit_resid()
        plan_out(NT - 1)
        emit_resid()

        return {"tensor": sT.n, "vector": sV.n, "scalar": sSc.n,
                "stores": [(s, s.n) for s in sSx]}

    def plan_qkv_phase(finA):
        ab.reset()
        af.reset()
        wq_a = ab.alloc(8192)
        wk_a = ab.alloc(8192)
        wv_a = ab.alloc(8192)
        xn4_a = [ab.alloc(4096) for _ in range(2)]
        xnT4_a = [ab.alloc(4096) for _ in range(2)]
        qks_a = [ab.alloc(8192) for _ in range(2)]
        vs_a = [ab.alloc(4096) for _ in range(2)]
        ones_a = ab.alloc(128)
        b1b_a = ab.alloc(3 * D)
        xb4_a = [af.alloc(4096) for _ in range(2)]
        bvt_a = af.alloc(D)
        bqc_a = af.alloc(8)
        bkc_a = af.alloc(8)
        b1_a = af.alloc(3 * D)
        st4_a = [[af.alloc(8) for _ in range(4)] for _ in range(2)]

        entry_t = [(sT, finA["tensor"])]
        entry_fv = [(sV, finA["vector"])]
        entry_st = list(finA["stores"])

        P.op("gpsimd", DMA(bfr(wq_a, 8192, "p (ec f) -> p ec f", ec=8),
                           lambda: WqT.rearrange("(ec p) f -> p ec f", p=128)),
             waits=entry_t, incs=[(sGW, 16)])
        P.op("gpsimd", DMA(bfr(wk_a, 8192, "p (ec f) -> p ec f", ec=8),
                           lambda: WkT.rearrange("(ec p) f -> p ec f", p=128)),
             incs=[(sGW, 16)])
        P.op("gpsimd", DMA(bfr(wv_a, 8192, "p (ec f) -> p ec f", ec=8),
                           lambda: WvT.rearrange("(ec p) f -> p ec f", p=128)),
             incs=[(sGW, 16)])
        m_w = sGW.n
        P.op("sync", DMA(fpv(b1_a, D, 0, 1), dram(bq)),
             waits=entry_fv + entry_st, incs=[(sLb, 16)])
        P.op("sync", DMA(fpv(b1_a + D, D, 0, 1), dram(bk)), incs=[(sLb, 16)])
        P.op("sync", DMA(fpv(b1_a + 2 * D, D, 0, 1), dram(bv)), incs=[(sLb, 16)])
        m_b = sLb.n
        P.op("gpsimd", lambda e: e.memset(bfv(ones_a, 128, 0, 1)(), 1.0),
             incs=[(sG, 1)])
        m_ones = sG.n
        P.op("vector", lambda e: e.tensor_copy(bfv(b1b_a, 3 * D, 0, 1)(),
                                               fpv(b1_a, 3 * D, 0, 1)()),
             waits=[(sLb, m_b)] + entry_fv, incs=[(sV, 1)])
        m_b1b = sV.n
        for bi, src_off in enumerate([0, D]):
            for fc in range(8):
                P.op("tensor", MM(psv("tr", bi * 8 + fc, bi * 8 + fc + 1),
                                  bfv(b1b_a + src_off + fc * 128, 128, 0, 1),
                                  bfv(ones_a, 1, 0, 1), True, True),
                     waits=[(sV, m_b1b), (sG, m_ones)] if (bi == 0 and fc == 0) else ())
        for q in range(2):
            P.op("tensor", MM(psv("out", q * 512, (q + 1) * 512),
                              bfv(ones_a, 128, 0, 1),
                              bfv(b1b_a + 2 * D + q * 512, 512, 0, 1), True, True),
                 incs=[(sT, 1)] if q == 1 else ())
        m_bmm = sT.n
        P.op("scalar", lambda e: e.activation(out=fpv(bqc_a, 8)(), in_=psv("tr", 0, 8)(),
                                              func=ACTF.Copy),
             waits=[(sT, m_bmm), (sSc, finA["scalar"])])
        P.op("scalar", lambda e: e.activation(out=fpv(bkc_a, 8)(), in_=psv("tr", 8, 16)(),
                                              func=ACTF.Copy))
        P.op("scalar", lambda e: e.activation(out=fpv(bvt_a, D)(),
                                              in_=psv("out", 0, 1024)(), func=ACTF.Copy),
             incs=[(sSc, 1)])
        m_bias = sSc.n

        ps_tr_free = [(sSc, m_bias), (sSc, m_bias)]
        ps_qk_free = [(sSc, m_bias), (sSc, m_bias)]
        ps_v_free = None
        v_store = [0, 0]
        qk_store = [0, 0]
        xnT_rd = [0, 0]
        for st in range(NST):
            par = st % 2
            lw = list(entry_st)
            if st >= 2:
                lw.append((sT, xnT_rd[par]))
            if st == 0:
                lw += entry_fv
            P.op("sync", DMA(fpr(xb4_a[par], 4096, "p (j v) -> p j v", j=4),
                             lambda st=st: x1d[st * 512:(st + 1) * 512, :]
                             .rearrange("(j p) v -> p j v", p=128)),
                 waits=lw, incs=[(sLx[par], 16)])
            m_x = sLx[par].n
            c = lambda j, i, par=par: fpv(st4_a[par][j] + i, 1)
            xv = lambda j, par=par: fpv(xb4_a[par] + j * 1024, D)
            xnv = lambda j, par=par: bfv(xn4_a[par] + j * 1024, D)
            for j in range(4):
                P.op("vector", lambda e, j=j, c=c, xv=xv: e.tensor_reduce(
                    out=c(j, 0)(), in_=xv(j)(), axis=mybir.AxisListType.X, op=ALU.add),
                     waits=[(sLx[par], m_x)] if j == 0 else ())
                P.op("scalar", lambda e, j=j, c=c, xv=xv, xnv=xnv: e.activation(
                    out=xnv(j)(), in_=xv(j)(), func=ACTF.Square,
                    accum_out=c(j, 1)()),
                     waits=[(sLx[par], m_x)] if j == 0 else (),
                     incs=[(sSc, 1)] if j == 3 else ())
            m_sq = sSc.n
            for j in range(4):
                P.op("vector", lambda e, j=j, c=c: e.tensor_scalar(
                    out=c(j, 2)(), in0=c(j, 0)(), scalar1=1.0 / D, scalar2=None,
                    op0=ALU.mult))
                P.op("vector", lambda e, j=j, c=c: e.tensor_scalar(
                    out=c(j, 3)(), in0=c(j, 1)(), scalar1=1.0 / D, scalar2=LN_EPS,
                    op0=ALU.mult, op1=ALU.add),
                     waits=[(sSc, m_sq)] if j == 0 else ())
            for j in range(4):
                P.op("vector", lambda e, j=j, c=c: e.tensor_tensor(
                    out=c(j, 4)(), in0=c(j, 2)(), in1=c(j, 2)(), op=ALU.mult),
                     incs=[(sV, 1)] if j == 3 else ())
            m_msq = sV.n
            for j in range(4):
                P.op("scalar", lambda e, j=j, c=c: e.activation(
                    out=c(j, 5)(), in_=c(j, 4)(), func=ACTF.Sqrt, scale=-1.0,
                    bias=c(j, 3)()), waits=[(sV, m_msq)] if j == 0 else (),
                     incs=[(sSc, 1)] if j == 3 else ())
            m_std = sSc.n
            for j in range(4):
                P.op("vector", lambda e, j=j, c=c: e.reciprocal(out=c(j, 6)(), in_=c(j, 5)()),
                     waits=[(sSc, m_std)] if j == 0 else ())
            wxn = [(sT, xnT_rd[par])] if st >= 2 else []
            for j in range(4):
                P.op("vector", lambda e, j=j, c=c, xv=xv, xnv=xnv: e.tensor_scalar(
                    out=xnv(j)(), in0=xv(j)(), scalar1=c(j, 2)(), scalar2=c(j, 6)(),
                    op0=ALU.subtract, op1=ALU.mult),
                     waits=wxn if j == 0 else (), incs=[(sV, 1)] if j == 3 else ())
            m_xn = sV.n
            if debug_taps and st == 0:
                for j in range(4):
                    P.op("gpsimd", lambda e, j=j: e.dma_start(
                        out=dbg[:, 15872 + j * 8:15880 + j * 8],
                        in_=fpv(st4_a[par][j], 8)(),
                        allow_slow_non_contiguous=True),
                         waits=[(sV, m_xn)] if j == 0 else (), incs=[(sG, 16)])
                P.op("gpsimd", DMA(lambda: dbg[:, 15000:15512],
                                   bfv(xn4_a[par], 512)), incs=[(sG, 16)])
                mgs = sG.n
                P.op("tensor", lambda e: e.nop(), waits=[(sG, mgs)])
            for j in range(4):
                for r in range(2):
                    for k in range(4):
                        dc = 4 * r + k
                        w = []
                        if k == 0:
                            w = [(sV, m_xn), ps_tr_free[r]]
                        P.op("tensor", MM(psv("tr", (r % 2) * 512 + k * 128,
                                              (r % 2) * 512 + (k + 1) * 128),
                                          bfv(xn4_a[par] + j * 1024 + dc * 128, 128),
                                          lambda: identh[:, :], True, True),
                             waits=w, incs=[(sT, 1)] if k == 3 else ())
                    m_tr = sT.n
                    for k in range(4):
                        dc = 4 * r + k
                        P.op("vector", lambda e, par=par, j=j, dc=dc, k=k, r=r:
                             e.tensor_copy(
                                 bfv(xnT4_a[par] + dc * 512 + j * 128, 128)(),
                                 psv("tr", (r % 2) * 512 + k * 128,
                                     (r % 2) * 512 + (k + 1) * 128)()),
                             waits=[(sT, m_tr)] if k == 0 else (),
                             incs=[(sV, 1)] if k == 3 else ())
                    ps_tr_free[r % 2] = (sV, sV.n)
            m_xnT = sV.n
            for pi, (w_a, bc_a, qoff, dstd) in enumerate(
                    [(wq_a, bqc_a, 0, qtd), (wk_a, bkc_a, 4096, ktd)]):
                for fc in range(8):
                    pp = fc % 2
                    w = [ps_qk_free[pp]]
                    if pi == 0 and fc == 0:
                        w += [(sV, m_xnT), (sGW, m_w)]
                        if st >= 2:
                            w.append((sSq[par], qk_store[par]))
                    for ec in range(8):
                        P.op("tensor", MM(psv("h", pp * 512, (pp + 1) * 512),
                                          bfv(w_a + ec * 1024 + fc * 128, 128),
                                          bfv(xnT4_a[par] + ec * 512, 512),
                                          ec == 0, ec == 7),
                             waits=w if ec == 0 else (),
                             incs=[(sT, 1)] if ec == 7 else ())
                    m_mm = sT.n
                    P.op("vector", lambda e, par=par, fc=fc, pp=pp, qoff=qoff,
                         bc_a=bc_a: e.tensor_scalar(
                             out=bfv(qks_a[par] + qoff + fc * 512, 512)(),
                             in0=psv("h", pp * 512, (pp + 1) * 512)(),
                             scalar1=fpv(bc_a + fc, 1)(), scalar2=None, op0=ALU.add),
                         waits=[(sT, m_mm)], incs=[(sV, 1)])
                    ps_qk_free[pp] = (sV, sV.n)
                m_proj = (sV, sV.n)
                P.op("scalar", DMA(lambda st=st, dstd=dstd:
                                   dstd[:, :, st * 512:(st + 1) * 512]
                                   .rearrange("fc p q -> p fc q"),
                                   bfr(qks_a[par] + qoff, 4096, "p (fc q) -> p fc q",
                                       fc=8)),
                     waits=[m_proj], incs=[(sSq[par], 16)])
                qk_store[par] = sSq[par].n
            for j in range(4):
                w = [(sV, m_xnT)]
                if ps_v_free is not None:
                    w.append(ps_v_free)
                else:
                    w += entry_fv
                for ec in range(8):
                    for q in range(2):
                        P.op("tensor", MM(psv("out", q * 512, (q + 1) * 512),
                                          bfv(xnT4_a[par] + ec * 512 + j * 128, 128),
                                          bfv(wv_a + ec * 1024 + q * 512, 512),
                                          ec == 0, ec == 7),
                             waits=w if (ec == 0 and q == 0) else (),
                             incs=[(sT, 1)] if (ec == 7 and q == 1) else ())
                m_mm = sT.n
                wv2 = [(sT, m_mm), (sSc, m_bias)]
                if st >= 2 and j == 0:
                    wv2.append((sSv[par], v_store[par]))
                P.op("vector", lambda e, par=par, j=j: e.tensor_tensor(
                    out=bfv(vs_a[par] + j * 1024, D)(), in0=psv("out", 0, 1024)(),
                    in1=fpv(bvt_a, D)(), op=ALU.add),
                     waits=wv2, incs=[(sV, 1)])
                ps_v_free = (sV, sV.n)
            m_v = sV.n
            P.op("scalar", DMA(lambda st=st: vd[st * 512:(st + 1) * 512, :]
                               .rearrange("(j p) v -> p j v", p=128),
                               bfr(vs_a[par], 4096, "p (j v) -> p j v", j=4)),
                 waits=[(sV, m_v)], incs=[(sSv[par], 16)])
            v_store[par] = sSv[par].n
            xnT_rd[par] = sT.n
        return {"tensor": sT.n, "vector": sV.n, "scalar": sSc.n,
                "stores": finA["stores"] + [(s, s.n) for s in sSq + sSv]}

    def plan_attn_phase(finB1, x2dst):
        ab.reset()
        af.reset()
        KD = 4
        kb_a = [ab.alloc(4096) for _ in range(KD)]
        vb_a = [ab.alloc(4096) for _ in range(KD)]
        qt_a = [ab.alloc(2048) for _ in range(2)]
        sc_a = [[ab.alloc(8192) for _ in range(2)] for _ in range(2)]
        tmp_a = [ab.alloc(8192), None]
        tmp_a[1] = tmp_a[0]
        x1g_a = [af.alloc(2048) for _ in range(2)]
        x2s_a = [af.alloc(D) for _ in range(2)]
        den_a = [af.alloc(2) for _ in range(2)]
        d2_a = [af.alloc(2) for _ in range(2)]
        rec_a = af.alloc(2)
        mx_a = af.alloc(512)

        entry_t = [(sT, finB1["tensor"])]
        entry_v = [(sV, finB1["vector"])]
        entry_sc = [(sSc, finB1["scalar"])]
        entry_st = list(finB1["stores"])

        P.op("gpsimd", lambda e: e.memset(fpv(mx_a, 512)(), 0.0),
             waits=entry_v + entry_sc, incs=[(sG, 1)])
        m_mxz = sG.n

        kld = {}
        vld = {}
        kcons = {}
        vcons = {}
        sc_evac = {}
        mask_done = {}
        trev = {}
        av_done = {}
        epi_done = {}
        x2_store = {}
        qt_ld = {}
        x1_ld = {}
        ps_s_free = [entry_sc[0], entry_sc[0]]
        ps_tr_free = [entry_sc[0], entry_sc[0]]
        kcnt = 0
        vcnt = 0

        def plan_kload(s16):
            nonlocal kcnt
            if kcnt < KD:
                w = entry_t + entry_st
            else:
                w = [(sT, kcons[kcnt - KD])]
            sl = sLk[kcnt % 2]
            P.op("scalar", DMA(bfr(kb_a[kcnt % KD], 4096, "p (fc k) -> p fc k", fc=8),
                             lambda s16=s16: ktd[:, :, s16 * 512:(s16 + 1) * 512]
                             .rearrange("fc p k -> p fc k")),
                 waits=w, incs=[(sl, 16)])
            kld[kcnt] = (sl, sl.n)
            kcnt += 1

        def plan_vload(s16):
            nonlocal vcnt
            if vcnt < KD:
                w = entry_t + entry_st
            else:
                w = [(sT, vcons[vcnt - KD])]
            sl = sLv[vcnt % 2]
            P.op("sync", DMA(bfr(vb_a[vcnt % KD], 4096, "p (kc v) -> p kc v", kc=4),
                             lambda s16=s16: vd[s16 * 512:(s16 + 1) * 512, :]
                             .rearrange("(kc p) v -> p kc v", p=128)),
                 waits=w, incs=[(sl, 16)])
            vld[vcnt] = (sl, sl.n)
            vcnt += 1

        def plan_qtload(g):
            w = entry_t + entry_st if g < 2 else [(sT, scores_mm_d[g - 2])]
            sl = sLq[g % 2]
            P.op("scalar", DMA(bfr(qt_a[g % 2], 2048, "p (fc q) -> p fc q", fc=8),
                             lambda g=g: qtd[:, :, g * 256:(g + 1) * 256]
                             .rearrange("fc p q -> p fc q")),
                 waits=w, incs=[(sl, 16)])
            qt_ld[g] = (sl, sl.n)
            if debug_taps and g == 0:
                P.op("gpsimd", DMA(lambda: dbg[:, 6144:8192], bfv(qt_a[0], 2048)),
                     waits=[qt_ld[0]], incs=[(sG, 16)])
                mgq = sG.n
                P.op("tensor", lambda e: e.nop(), waits=[(sG, mgq)])

        def plan_x1load(g):
            if g < 2:
                w = entry_v + list(finB1["stores"])
            else:
                w = [(sV, epi_done[(g - 2, 1)])]
            sl = sLg[g % 2]
            P.op("sync", DMA(fpr(x1g_a[g % 2], 2048, "p (j v) -> p j v", j=2),
                             lambda g=g: x1d[g * 256:(g + 1) * 256, :]
                             .rearrange("(j p) v -> p j v", p=128)),
                 waits=w, incs=[(sl, 16)])
            x1_ld[g] = (sl, sl.n)

        scores_mm_d = {}

        def plan_scores(g):
            for s16 in range(16):
                ks = g * 16 + s16
                for j in range(2):
                    pp = (s16 * 2 + j) % 2
                    w = [kld[ks], ps_s_free[pp], qt_ld[g]]
                    if s16 == 0 and j == 0:
                        if g >= 2:
                            w.append((sT, av_done[g - 2]))
                        else:
                            w += entry_v + entry_sc + entry_st
                    for fc in range(8):
                        P.op("tensor", MM(psv("h", pp * 512, (pp + 1) * 512),
                                          bfv(qt_a[g % 2] + fc * 256 + j * 128, 128),
                                          bfv(kb_a[ks % KD] + fc * 512, 512),
                                          fc == 0, fc == 7),
                             waits=w if fc == 0 else (),
                             incs=[(sT, 1)] if fc == 7 else ())
                    if j == 1:
                        kcons[ks] = sT.n
                    m_mm = sT.n
                    P.op("scalar", lambda e, g=g, j=j, s16=s16, pp=pp: e.activation(
                        out=bfv(sc_a[g % 2][j] + s16 * 512, 512)(),
                        in_=psv("h", pp * 512, (pp + 1) * 512)(), func=ACTF.Copy),
                         waits=[(sT, m_mm)], incs=[(sSc, 1)])
                    ps_s_free[pp] = (sSc, sSc.n)
                    sc_evac[(g, j)] = sSc.n
                if s16 + KD < 16:
                    plan_kload(s16 + KD)
            scores_mm_d[g] = sT.n

        def plan_topk(g, j):
            sc_v = bfv(sc_a[g % 2][j], 8192)
            tmp_v = bfv(tmp_a[j], 8192)
            if debug_taps and (g, j) == (21, 1):
                P.op("vector", lambda e: e.nop(), waits=[(sSc, sc_evac[(g, j)])],
                     incs=[(sV, 1)])
                m0 = sV.n
                P.op("gpsimd", DMA(lambda: dbg[:, 8192:15360], bfv(sc_a[g % 2][j], 7168)),
                     waits=[(sV, m0)], incs=[(sG, 16)])
                P.op("gpsimd", DMA(lambda: dbg[:, 0:2048], bfv(qt_a[g % 2], 2048)),
                     incs=[(sG, 16)])
                P.op("gpsimd", DMA(lambda: dbg[:, 2048:6144], bfv(kb_a[0], 4096)),
                     incs=[(sG, 16)])
                mg0 = sG.n
                P.op("vector", lambda e: e.nop(), waits=[(sG, mg0)])
            P.op("vector", lambda e: e.max(out=fpv(mx_a, 8)(), in_=sc_v()),
                 waits=[(sSc, sc_evac[(g, j)]), (sG, m_mxz)])
            P.op("vector", lambda e: e.drain())
            if g == 0 and j == 0:
                wmr = entry_v + entry_st
            elif j == 0:
                wmr = [(sSc, trev[(g - 1, 1)])]
            else:
                wmr = [(sSc, trev[(g, 0)])]
            P.op("vector", lambda e: e.match_replace(out=tmp_v(),
                                                     in_to_replace=fpv(mx_a, 8)(),
                                                     in_values=sc_v(), imm_value=0.0),
                 waits=wmr)
            P.op("vector", lambda e: e.drain())
            if debug_taps and (g, j) == (21, 1):
                P.op("vector", lambda e: e.nop(), incs=[(sV, 1)])
                m1 = sV.n
                P.op("gpsimd", DMA(lambda: dbg[:, 0:7168], bfv(tmp_a, 7168)),
                     waits=[(sV, m1)], incs=[(sG, 16)])
                mg = sG.n
                P.op("vector", lambda e: e.nop(), waits=[(sG, mg)])
            for rnd in range(1, 4):
                P.op("vector", lambda e, rnd=rnd: e.max(out=fpv(mx_a + rnd * 128, 8)(),
                                                        in_=tmp_v()))
                P.op("vector", lambda e: e.drain())
                if rnd < 3:
                    P.op("vector", lambda e, rnd=rnd: e.match_replace(
                        out=tmp_v(), in_to_replace=fpv(mx_a + rnd * 128, 8)(),
                        in_values=tmp_v(), imm_value=0.0))
                    P.op("vector", lambda e: e.drain())
            P.op("vector", lambda e, g=g, j=j: e.tensor_reduce(
                out=fpv(den_a[g % 2] + j, 1)(), in_=fpv(mx_a, 512)(),
                axis=mybir.AxisListType.X, op=ALU.add), incs=[(sV, 1)])
            m_top = sV.n
            P.op("vector", lambda e: e.tensor_scalar(
                out=tmp_v(), in0=sc_v(), scalar1=fpv(mx_a + 3 * 128 + 7, 1)(),
                scalar2=None, op0=ALU.is_ge))
            P.op("vector", lambda e: e.tensor_tensor(out=tmp_v(), in0=tmp_v(),
                                                     in1=sc_v(), op=ALU.mult),
                 incs=[(sV, 1)])
            mask_done[(g, j)] = (sV, sV.n)
            if debug_taps and (g, j) == (21, 999):
                m = mask_done[(g, j)]
                P.op("gpsimd", DMA(lambda: dbg[:, 0:8192], bfv(sc_a[g % 2][j], 8192)),
                     waits=[(sV, m)], incs=[(sG, 16)])
                P.op("gpsimd", DMA(lambda: dbg[:, 8192:15360], bfv(tmp_a, 7168)),
                     incs=[(sG, 16)])
                P.op("gpsimd", DMA(lambda: dbg[:, 15360:15872], fpv(mx_a, 512)),
                     incs=[(sG, 16)])
                P.op("gpsimd", lambda e: e.dma_start(
                    out=dbg[:, 15872:15873], in_=fpv(den_a[g % 2] + j, 1)(),
                    allow_slow_non_contiguous=True), incs=[(sG, 16)])

        def plan_tr(g, j):
            for r in range(16):
                for k in range(4):
                    kc = 4 * r + k
                    w = []
                    if k == 0:
                        w = [mask_done[(g, j)], ps_tr_free[r % 2]]
                    P.op("tensor", MM(psv("tr", (r % 2) * 512 + k * 128,
                                          (r % 2) * 512 + (k + 1) * 128),
                                      bfv(tmp_a[j] + kc * 128, 128),
                                      lambda: identh[:, :], True, True),
                         waits=w, incs=[(sT, 1)] if k == 3 else ())
                m_tr = sT.n
                P.op("scalar", lambda e, g=g, j=j, r=r: e.activation(
                    out=bfv(sc_a[g % 2][j] + r * 512, 512)(),
                    in_=psv("tr", (r % 2) * 512, (r % 2) * 512 + 512)(),
                    func=ACTF.Copy),
                     waits=[(sT, m_tr)], incs=[(sSc, 1)])
                ps_tr_free[r % 2] = (sSc, sSc.n)
            trev[(g, j)] = sSc.n

        def plan_av(g):
            for i in range(KD):
                plan_vload(i)
            for s16 in range(16):
                vs = g * 16 + s16
                for j in range(2):
                    w = [vld[vs]]
                    if s16 == 0:
                        w.append((sSc, trev[(g, j)]))
                        if g >= 1:
                            w.append((sV, epi_done[(g - 1, j)]))
                        else:
                            w += entry_v + entry_sc
                    for kc4 in range(4):
                        gkc = s16 * 4 + kc4
                        for hf in range(2):
                            if j == 0:
                                o = psv("out", hf * 512, (hf + 1) * 512)
                            else:
                                o = psv("h", 1024 + hf * 512, 1024 + (hf + 1) * 512)
                            P.op("tensor", MM(o,
                                              bfv(sc_a[g % 2][j] + gkc * 128, 128),
                                              bfv(vb_a[vs % KD] + kc4 * 1024 + hf * 512,
                                                  512),
                                              gkc == 0, gkc == 63),
                                 waits=w if (kc4 == 0 and hf == 0) else (),
                                 incs=[(sT, 1)] if (kc4 == 3 and hf == 1 and j == 1)
                                 else ())
                vcons[vs] = sT.n
                if s16 + KD < 16:
                    plan_vload(s16 + KD)
            av_done[g] = sT.n

        d2_ready = {}

        def plan_epi_d2(g):
            for j in range(2):
                P.op("scalar", lambda e, g=g, j=j: e.activation(
                    out=fpv(d2_a[g % 2] + j, 1)(), in_=fpv(den_a[g % 2] + j, 1)(),
                    func=ACTF.Identity, bias=eps10v()),
                     waits=[mask_done[(g, j)], (sG, 2)],
                     incs=[(sSc, 1)] if j == 1 else ())
            d2_ready[g] = sSc.n

        def plan_epi(g):
            m_d2 = d2_ready[g]
            for j in range(2):
                P.op("vector", lambda e, g=g, j=j: e.reciprocal(
                    out=fpv(rec_a + j, 1)(), in_=fpv(d2_a[g % 2] + j, 1)()),
                     waits=[(sSc, m_d2)] if j == 0 else ())
            for j in range(2):
                w = [(sT, av_done[g]), x1_ld[g]]
                if g >= 2:
                    w.append((sSw[j], x2_store[(g - 2, j)]))
                if j == 0:
                    o = psv("out", 0, 1024)
                else:
                    o = psv("h", 1024, 2048)
                P.op("vector", lambda e, g=g, j=j, o=o: e.scalar_tensor_tensor(
                    out=fpv(x2s_a[j], D)(), in0=o(), scalar=fpv(rec_a + j, 1)(),
                    in1=fpv(x1g_a[g % 2] + j * 1024, D)(),
                    op0=ALU.mult, op1=ALU.add),
                     waits=w, incs=[(sV, 1)])
                epi_done[(g, j)] = sV.n
                P.op("scalar", DMA(lambda g=g, j=j:
                                   x2dst[g * 256 + j * 128:g * 256 + (j + 1) * 128, :],
                                   fpv(x2s_a[j], D)),
                     waits=[(sV, epi_done[(g, j)])], incs=[(sSw[j], 16)])
                x2_store[(g, j)] = sSw[j].n

        for g in range(NG + 1):
            if g == 0:
                for i in range(KD):
                    plan_kload(i)
                plan_qtload(0)
                plan_x1load(0)
                plan_scores(0)
            if g >= 1:
                plan_tr(g - 1, 1)
                plan_av(g - 1)
            if g + 1 < NG:
                for i in range(KD):
                    plan_kload(i)
                plan_qtload(g + 1)
            if g < NG:
                plan_topk(g, 0)
                plan_tr(g, 0)
            if g + 1 < NG:
                plan_scores(g + 1)
            if g >= 1:
                plan_epi(g - 1)
            if g + 1 < NG:
                plan_x1load(g + 1)
            if g < NG:
                plan_topk(g, 1)
                plan_epi_d2(g)

        return {"tensor": sT.n, "vector": sV.n, "scalar": sSc.n,
                "stores": [(s, s.n) for s in sSw]}

    P.op("gpsimd", lambda e: e.memset(epsv(), LN_EPS), incs=[(sG, 1)])
    P.op("gpsimd", lambda e: e.memset(eps10v(), 1e-10), incs=[(sG, 1)])

    no_w = {}
    if stop_after == "ffn1":
        plan_ffn_phase("A", x_in, out, no_w)
    else:
        finA = plan_ffn_phase("A", x_in, x1d, no_w)
        finB1 = plan_qkv_phase(finA)
        if stop_after == "attn":
            plan_attn_phase(finB1, out)
        else:
            finB2 = plan_attn_phase(finB1, x2d)
            entry_C = {
                "gpsimd": ((sT, finB2["tensor"]), (sSc, finB2["scalar"]),
                           (sV, finB2["vector"])),
                "vector": ((sV, finB2["vector"]),),
                "tensor": ((sSc, finB2["scalar"]),),
                "scalar": ((sV, finB2["vector"]),),
                "sync": tuple([(sV, finB2["vector"])] + list(finB2["stores"])),
            }
            plan_ffn_phase("C", x2d, out, entry_C)

    if debug_taps:
        wfin = [(s, s.n) for s in sSx + sSq + sSv + sSw if s.n > 0] \
            + [(sT, sT.n), (sV, sV.n)]
        P.op("gpsimd", DMA(lambda: dbgx[:, :], dram(x1d)), waits=wfin,
             incs=[(sG, 16)])
        P.op("gpsimd", DMA(lambda: dbgq[:, :, :], dram(qtd)), incs=[(sG, 16)])
        P.op("gpsimd", DMA(lambda: dbgk[:, :, :], dram(ktd)), incs=[(sG, 16)])
        P.op("gpsimd", DMA(lambda: dbgv[:, :], dram(vd)), incs=[(sG, 16)])
        P.op("sync", lambda e: e.nop(), waits=[(sG, sG.n)])
    P.op("sync", lambda e: e.nop(),
         waits=[(s, s.n) for s in sSx + sSq + sSv + sSw if s.n > 0])

    import contextlib
    with contextlib.ExitStack() as ctx:
        ab.h = ctx.enter_context(nc.sbuf_tensor("arena_bf", [128, BF_COLS], BF16))
        af.h = ctx.enter_context(nc.sbuf_tensor("arena_fp", [128, FP_COLS], F32))
        identh_t = ctx.enter_context(nc.sbuf_tensor("identsb", [128, 128], BF16))
        consts_t = ctx.enter_context(nc.sbuf_tensor("constsb", [128, 2], F32))
        _consts_holder[0] = consts_t
        ps["h"] = ctx.enter_context(nc.psum_tensor("ps_h", [128, 2048], F32))
        ps["out"] = ctx.enter_context(nc.psum_tensor("ps_out", [128, 1024], F32))
        ps["tr"] = ctx.enter_context(nc.psum_tensor("ps_tr", [128, 1024], F32))
        semh = {s: ctx.enter_context(nc.semaphore(s.name)) for s in P.sems}
        _ident_holder[0] = identh_t

        with nc.Block() as block:
            def emit_for(eng_name):
                def runner(eng):
                    waited = {}
                    if eng_name == "gpsimd":
                        eng.dma_start(out=identh_t[:, :], in_=identd).then_inc(
                            semh[sGW], 16)
                    if eng_name == "tensor":
                        eng.wait_ge(semh[sGW], 16)
                    for engine, fn, waits, incs in P.ops:
                        if engine != eng_name:
                            continue
                        for s, v in waits:
                            v2 = v + (16 if s is sGW else 0)
                            if waited.get(s.name, -1) < v2:
                                eng.wait_ge(semh[s], v2)
                                waited[s.name] = v2
                        inst = fn(eng)
                        for s, k in incs:
                            inst.then_inc(semh[s], k)
                return runner

            block.sync(emit_for("sync"))
            block.scalar(emit_for("scalar"))
            block.vector(emit_for("vector"))
            block.tensor(emit_for("tensor"))
            block.gpsimd(emit_for("gpsimd"))

    return nc


def kernel(L0_attn_x, L0_attn_Wq, L0_attn_bq, L0_attn_Wk, L0_attn_bk,
           L0_attn_Wv, L0_attn_bv, L0_ffn_expert_sel, L0_ffn_keys, L0_ffn_values):
    x = np.ascontiguousarray(np.asarray(L0_attn_x, dtype=np.float32))
    B = x.shape[0]
    common = {
        "WqT": np.ascontiguousarray(np.asarray(L0_attn_Wq, np.float32).T),
        "WkT": np.ascontiguousarray(np.asarray(L0_attn_Wk, np.float32).T),
        "WvT": np.ascontiguousarray(np.asarray(L0_attn_Wv, np.float32).T),
        "bq": np.ascontiguousarray(np.asarray(L0_attn_bq, np.float32)[None, :]),
        "bk": np.ascontiguousarray(np.asarray(L0_attn_bk, np.float32)[None, :]),
        "bv": np.ascontiguousarray(np.asarray(L0_attn_bv, np.float32)[None, :]),
        "esel": np.ascontiguousarray(np.asarray(L0_ffn_expert_sel, np.float32)),
        "keys": np.ascontiguousarray(np.asarray(L0_ffn_keys, np.float32)),
        "values": np.ascontiguousarray(np.asarray(L0_ffn_values, np.float32)),
        "ident": np.eye(128, dtype=ml_dtypes.bfloat16),
    }
    nc = build("full")
    in_maps = [dict(common, x=np.ascontiguousarray(x[i])) for i in range(B)]
    res = run_bass_kernel_spmd(nc, in_maps, core_ids=list(range(B)))
    return np.stack([np.asarray(res.results[i]["out"]) for i in range(B)], axis=0)
